# version 14
# baseline (speedup 1.0000x reference)
import math
import numpy as np
import ml_dtypes
from contextlib import ExitStack
import concourse.bass as bass
import concourse.mybir as mybir
from concourse.bass_utils import run_bass_kernel_spmd

F32 = mybir.dt.float32
BF16 = mybir.dt.bfloat16
U32 = mybir.dt.uint32
ALU = mybir.AluOpType
AF = mybir.ActivationFunctionType
AX = mybir.AxisListType.X
bfdt = ml_dtypes.bfloat16

BLK = 512
EPS = 1e-5
CSHIFT = 24.0
SC = 1.0 / math.sqrt(128.0)
CLIP = 1024.0 * math.sqrt(128.0)
NEG = -1e30

ENGINES = ('sync', 'gpsimd', 'tensor', 'vector', 'scalar')


class Cfg:
    def __init__(self, D=2048, S=4096, H=16, HID=1024, E=8, B=4):
        self.D, self.S, self.H, self.HID, self.E, self.B = D, S, H, HID, E, B
        self.DC = D // 128
        self.HC = HID // 128
        self.NB = S // BLK
        self.NS = self.NB // 2
        self.OWN = self.NS * BLK
        self.T8 = S // 512
        pat = ([0, 1, 1, 0] * ((self.NB + 3) // 4))[:self.NB]
        self.own = {0: [i for i, v in enumerate(pat) if v == 0],
                    1: [i for i, v in enumerate(pat) if v == 1]}
        assert len(self.own[0]) == self.NS and len(self.own[1]) == self.NS
        self.KT = [4 * (max(self.own[0][s], self.own[1][s]) + 1)
                   for s in range(self.NS)]


def _slopes(n):
    def pow2(m):
        start = 2 ** (-2 ** (-(math.log2(m) - 3)))
        return [start * start ** i for i in range(m)]
    if math.log2(n).is_integer():
        return pow2(n)
    c = 2 ** math.floor(math.log2(n))
    return pow2(c) + _slopes(2 * c)[0::2][:n - c]


def make_core_consts(cfg, parity):
    H, NS = cfg.H, cfg.NS
    sl = np.array(_slopes(H), dtype=np.float64)
    own = cfg.own[parity]
    slopes_rep = np.repeat(sl, 128).astype(bfdt)[None, :]
    qrow = np.zeros((1, NS * BLK), np.float64)
    for s, g in enumerate(own):
        q = g * BLK + np.arange(BLK)
        qrow[0, s * BLK:(s + 1) * BLK] = -math.sqrt(128.0) * q
    qrow = qrow.astype(bfdt)
    alibi_k = np.full((128, H * 128), NEG, np.float32)
    for h in range(H):
        for s in range(NS):
            kmax_t = 4 * (own[s] + 1)
            for kc in range(cfg.KT[s]):
                col = h * 128 + s * 32 + kc
                if kc < kmax_t:
                    j = kc * 128 + np.arange(128)
                    alibi_k[:, col] = (sl[h] * j - CSHIFT).astype(np.float32)
    flags = np.ones((128, 128), np.float32)
    for s in range(NS):
        for kc in range(cfg.KT[s]):
            if kc // 4 == own[s]:
                flags[:, s * 32 + kc] = 0.0
    canon = np.zeros((128, 4 * 512), np.float32)
    for o in range(4):
        for sub in range(4):
            blkv = canon[:, o * 512 + sub * 128: o * 512 + (sub + 1) * 128]
            if sub < o:
                blkv[:] = NEG
            elif sub == o:
                jj = np.arange(128)[:, None]
                qq = np.arange(128)[None, :]
                blkv[:] = np.where(qq >= jj, 0.0, NEG).astype(np.float32)
    flags = 1.0 - flags
    return dict(slopes_rep=slopes_rep, qrow=qrow, alibi_k=alibi_k,
                flags=flags, canon=canon,
                blkidx=np.array([own], dtype=np.uint32),
                identB=np.eye(128, dtype=bfdt),
                identF=np.eye(128, dtype=np.float32))


class Op:
    __slots__ = ('oid', 'eng', 'fn', 'deps', 'dma', 'fence',
                 'inc_val', 'tag', 'semkey', 'sem')

    def __init__(self, oid, eng, fn, deps, dma, fence, tag):
        self.oid, self.eng, self.fn = oid, eng, fn
        self.deps, self.dma, self.fence = deps, dma, fence
        self.inc_val = None
        self.tag = tag


class Sched:

    def __init__(self, nc):
        self.nc = nc
        self.csem = {e: nc.alloc_semaphore(f'cs_{e}') for e in ENGINES}
        self.ccnt = {e: 0 for e in ENGINES}
        self.dsems = {}
        self._reset()

    def _reset(self):
        self.ops = []
        self.last_w = {}
        self.readers = {}

    def _dsem(self, key):
        if key not in self.dsems:
            self.dsems[key] = [self.nc.alloc_semaphore(f'dk{len(self.dsems)}'),
                               0]
        return self.dsems[key]

    def op(self, eng, fn, reads=(), writes=(), dma=False, fence=False,
           deps=(), tag=''):
        dep = set(deps)
        for k in reads:
            if k in self.last_w:
                dep.add(self.last_w[k])
        for k in writes:
            for r in self.readers.get(k, ()):
                dep.add(r)
            if k in self.last_w:
                p = self.ops[self.last_w[k]]
                if not (dma and p.dma):
                    dep.add(self.last_w[k])
        oid = len(self.ops)
        o = Op(oid, eng, fn, sorted(dep), dma, fence, tag)
        if dma:
            o.semkey = writes[0] if writes else ('uniq', oid)
        self.ops.append(o)
        for k in reads:
            self.readers.setdefault(k, []).append(oid)
        for k in writes:
            self.last_w[k] = oid
            self.readers[k] = []
        return oid

    def drain(self):
        dmas = tuple(o.oid for o in self.ops if o.dma)
        if dmas:
            self.op('sync', lambda E: E.nop(), deps=dmas, tag='drain')
            self.op('gpsimd', lambda E: E.nop(), deps=dmas, tag='drain')

    def emit(self, name):
        ops = self.ops
        needed = set()
        for o in ops:
            for d in o.deps:
                p = ops[d]
                if not (p.eng == o.eng == 'tensor' and not p.fence
                        and not p.dma):
                    needed.add(d)
        for o in ops:
            if o.dma:
                ds = self._dsem(o.semkey)
                ds[1] += 16
                o.inc_val = ds[1]
                o.sem = ds[0]
            elif o.oid in needed:
                self.ccnt[o.eng] += 1
                o.inc_val = self.ccnt[o.eng]
                o.sem = self.csem[o.eng]
        per_eng = {e: [o for o in ops if o.eng == e] for e in ENGINES}
        nc = self.nc
        with nc.Block(name) as blk:
            for eng in ENGINES:
                eops = per_eng[eng]
                if not eops:
                    continue

                def body(E, eops=eops, eng=eng):
                    hw = {}
                    for o in eops:
                        want = {}
                        for d in o.deps:
                            p = ops[d]
                            if p.inc_val is None:
                                continue
                            if (p.eng == eng == 'tensor'
                                    and not (p.fence or p.dma)):
                                continue
                            cur = want.get(p.sem.name)
                            if cur is None or cur[1] < p.inc_val:
                                want[p.sem.name] = (p.sem, p.inc_val)
                        for name, (sem, val) in want.items():
                            if hw.get(name, -1) < val:
                                E.wait_ge(sem, val)
                                hw[name] = val
                        inst = o.fn(E)
                        if o.inc_val is not None:
                            inst.then_inc(o.sem, 16 if o.dma else 1)
                getattr(blk, eng)(body)
        self._reset()



def build_nc(cfg, phase_limit=99, dbg=False):
    nc = bass.Bass('TRN2', target_bir_lowering=False,
                   detect_race_conditions=False)
    D, S, H, HID, E = cfg.D, cfg.S, cfg.H, cfg.HID, cfg.E
    DC, HC, NS, OWN, T8 = cfg.DC, cfg.HC, cfg.NS, cfg.OWN, cfg.T8
    NSB = NS * BLK

    def din(name, shape, dt=F32):
        return nc.dram_tensor(name, list(shape), dt, kind='ExternalInput').ap()

    xT = din('xT', [D, S])
    xTo = din('xTo', [D, OWN])
    wqT = din('wqT', [D, H * 128])
    wkT = din('wkT', [D, H * 128])
    wvT = din('wvT', [D, H * 128])
    woT = din('woT', [H * 128, D])
    gateT = din('gateT', [D, E])
    ebias = din('ebias', [1, E])
    w1T = din('w1T', [E, D, HID])
    w3T = din('w3T', [E, D, HID])
    w2T = din('w2T', [E, HID, D])
    sw1T = din('sw1T', [D, HID])
    sw3T = din('sw3T', [D, HID])
    sw2T = din('sw2T', [HID, D])
    slopes_i = din('slopes_rep', [1, H * 128], BF16)
    qrow_i = din('qrow', [1, NSB], BF16)
    alibik_i = din('alibi_k', [128, H * 128])
    flags_i = din('flags', [128, 128])
    canon_i = din('canon', [128, 4 * 512])
    blkidx_i = din('blkidx', [1, NS], U32)
    identB_i = din('identB', [128, 128], BF16)
    identF_i = din('identF', [128, 128])

    out_t = nc.dram_tensor('out_t', [D, OWN], F32, kind='ExternalOutput').ap()

    KI = 'ExternalOutput' if dbg else 'Internal'
    stats_d = nc.dram_tensor('stats_d', [2, S], F32, kind=KI).ap()
    wT_d = nc.dram_tensor('wT_d', [E, S], F32, kind=KI).ap()
    xn_d = nc.dram_tensor('xn_d', [D, S], BF16, kind=KI).ap()
    qT_d = nc.dram_tensor('qT_d', [H * 128, S], BF16, kind=KI).ap()
    kT_d = nc.dram_tensor('kT_d', [H * 128, S], BF16, kind=KI).ap()
    vtok_d = nc.dram_tensor('vtok_d', [S, H * 128], BF16, kind=KI).ap()
    qTo_d = nc.dram_tensor('qTo_d', [H * 128, OWN], BF16, kind=KI).ap()
    xno_d = nc.dram_tensor('xno_d', [D, OWN], BF16, kind=KI).ap()
    wTo_d = nc.dram_tensor('wTo_d', [E, OWN], F32, kind=KI).ap()
    attn_d = nc.dram_tensor('attn_d', [D, OWN], BF16, kind=KI).ap()

    ones_r = nc.alloc_sbuf_tensor('ones_r', [1, 128], F32)
    ones_rb = nc.alloc_sbuf_tensor('ones_rb', [1, 128], BF16)
    ones_c = nc.alloc_sbuf_tensor('ones_c', [128, 1], F32)
    ones_cb = nc.alloc_sbuf_tensor('ones_cb', [128, 1], BF16)
    identB = nc.alloc_sbuf_tensor('identB_s', [128, 128], BF16)
    identF = nc.alloc_sbuf_tensor('identF_s', [128, 128], F32)
    blki_s = nc.alloc_sbuf_tensor('blki_s', [1, NS], U32)
    eb_row = nc.alloc_sbuf_tensor('eb_row', [1, E], F32)
    ebias_t = nc.alloc_sbuf_tensor('ebias_t', [128, E], F32)

    sch = Sched(nc)
    OP = sch.op
    blk_sv = []

    with ExitStack() as _st1:
        xb = _st1.enter_context(nc.sbuf_tensor('p0_xb', [128, 512 * 3], F32))
        sqb = _st1.enter_context(nc.sbuf_tensor('p0_sq', [128, 512 * 2], F32))
        gtb = _st1.enter_context(nc.sbuf_tensor('p0_gt', [128, E * 2], F32))
        rows = _st1.enter_context(nc.sbuf_tensor('p0_rows', [1, 512 * 7], F32))
        r_all = _st1.enter_context(nc.sbuf_tensor('p0_rall', [1, S], F32))
        mr_all = _st1.enter_context(nc.sbuf_tensor('p0_mrall', [1, S], F32))
        raw_sb = _st1.enter_context(nc.sbuf_tensor('p0_raws', [E, 512], F32))
        sraw_sb = _st1.enter_context(nc.sbuf_tensor('p0_srw', [E, 512], F32))
        sig_sb = _st1.enter_context(nc.sbuf_tensor('p0_sig', [E, 512], F32))
        G_sb = _st1.enter_context(nc.sbuf_tensor('p0_Gs', [1, E], F32))
        tt = _st1.enter_context(nc.sbuf_tensor('p0_tt', [128, 8 * 6], F32))
        m1c = _st1.enter_context(nc.sbuf_tensor('p0_m1', [128, 1], F32))
        m2c = _st1.enter_context(nc.sbuf_tensor('p0_m2', [128, 1], F32))
        wT_sb = _st1.enter_context(nc.sbuf_tensor('p0_wT', [E, S], F32))
        pSX = _st1.enter_context(nc.psum_tensor('p0_sx', [1, 512], F32))
        pSXQ = _st1.enter_context(nc.psum_tensor('p0_sxq', [1, 512], F32))
        pRAW = _st1.enter_context(nc.psum_tensor('p0_raw', [E, 512], F32))
        pG = _st1.enter_context(nc.psum_tensor('p0_pG', [1, E], F32))
        pEB = _st1.enter_context(nc.psum_tensor('p0_eb', [128, E], F32))
        pRB8 = _st1.enter_context(nc.psum_tensor('p0_rb8', [E, 512], F32))
        pTPa = _st1.enter_context(nc.psum_tensor('p0_tpa', [128, 8], F32))
        pTPb = _st1.enter_context(nc.psum_tensor('p0_tpb', [8, 128], F32))

        m_row = rows[0:1, 0:512]; msq_row = rows[0:1, 512:1024]
        m2_row = rows[0:1, 1024:1536]; var_row = rows[0:1, 1536:2048]
        sd_row = rows[0:1, 2048:2560]; negm_row = rows[0:1, 2560:3072]
        rcp_row = rows[0:1, 3072:3584]
        st_t = tt[:, 0:8]; sb_t = tt[:, 8:16]; eq_t = tt[:, 16:24]
        sbx_t = tt[:, 24:32]; top_t = tt[:, 32:40]; w_t = tt[:, 40:48]

        OP('sync', lambda Ee: Ee.dma_start(identB[:], identB_i),
           writes=['identB'], dma=True)
        OP('sync', lambda Ee: Ee.dma_start(identF[:], identF_i),
           writes=['identF'], dma=True)
        OP('sync', lambda Ee: Ee.dma_start(blki_s[:], blkidx_i),
           writes=['blki'], dma=True)
        OP('sync', lambda Ee: Ee.dma_start(eb_row[:], ebias),
           writes=['ebrow'], dma=True)

        def _vload(Ee):
            for s in range(NS):
                blk_sv.append(Ee.value_load(blki_s[0:1, s:s + 1]))
            return Ee.nop()
        OP('sync', _vload, reads=['blki'], tag='vload')

        OP('vector', lambda Ee: Ee.memset(ones_r[:], 1.0), writes=['ones_r'])
        OP('vector', lambda Ee: Ee.memset(ones_rb[:], 1.0), writes=['ones_rb'])
        OP('vector', lambda Ee: Ee.memset(ones_c[:], 1.0), writes=['ones_c'])
        OP('vector', lambda Ee: Ee.memset(ones_cb[:], 1.0), writes=['ones_cb'])
        OP('tensor', lambda Ee: Ee.matmul(pEB[:], ones_r[0:1, :],
                                        eb_row[0:1, 0:E],
                                        start=True, stop=True),
           reads=['ones_r', 'ebrow'], writes=['pEB'])
        OP('vector', lambda Ee: Ee.tensor_copy(ebias_t[:], pEB[:]),
           reads=['pEB'], writes=['ebias_t'])

        for t in range(T8):
            for d in range(DC):
                n = t * DC + d
                xs = xb[:, (n % 3) * 512:(n % 3 + 1) * 512]
                sq = sqb[:, (n % 2) * 512:(n % 2 + 1) * 512]
                gs = gtb[:, (n % 2) * E:(n % 2 + 1) * E]
                OP('sync', lambda Ee, xs=xs, d=d, t=t: Ee.dma_start(
                    xs, xT[d * 128:(d + 1) * 128, t * 512:(t + 1) * 512]),
                   writes=[('xb', n % 3)], dma=True)
                OP('gpsimd', lambda Ee, gs=gs, d=d: Ee.dma_start(
                    gs, gateT[d * 128:(d + 1) * 128, :]),
                   writes=[('gt', n % 2)], dma=True)
                OP('scalar', lambda Ee, sq=sq, xs=xs: Ee.activation(
                    sq, xs, AF.Square),
                   reads=[('xb', n % 3)], writes=[('sq', n % 2)])
                OP('tensor', lambda Ee, xs=xs, d=d: Ee.matmul(
                    pSX[:], ones_c[:], xs, start=(d == 0), stop=(d == DC - 1)),
                   reads=[('xb', n % 3), 'ones_c'], writes=['pSX'])
                OP('tensor', lambda Ee, sq=sq, d=d: Ee.matmul(
                    pSXQ[:], ones_c[:], sq, start=(d == 0),
                    stop=(d == DC - 1)),
                   reads=[('sq', n % 2), 'ones_c'], writes=['pSXQ'])
                OP('tensor', lambda Ee, xs=xs, gs=gs, d=d: Ee.matmul(
                    pRAW[:], gs, xs, start=(d == 0), stop=(d == DC - 1)),
                   reads=[('xb', n % 3), ('gt', n % 2)], writes=['pRAW'])
                if t == 0:
                    OP('tensor', lambda Ee, gs=gs, d=d: Ee.matmul(
                        pG[:], ones_c[:], gs,
                        start=(d == 0), stop=(d == DC - 1)),
                       reads=[('gt', n % 2), 'ones_c'], writes=['pG'])
            if t == 0:
                OP('vector', lambda Ee: Ee.tensor_copy(G_sb[:], pG[:]),
                   reads=['pG'], writes=['G'])
            OP('vector', lambda Ee: Ee.tensor_scalar(m_row, pSX[0:1, :],
                                                   1.0 / D, None, ALU.mult),
               reads=['pSX'], writes=['m_row'])
            OP('vector', lambda Ee: Ee.tensor_scalar(msq_row, pSXQ[0:1, :],
                                                   1.0 / D, None, ALU.mult),
               reads=['pSXQ'], writes=['msq_row'])
            OP('vector', lambda Ee: Ee.tensor_scalar(negm_row, m_row, -1.0,
                                                   None, ALU.mult),
               reads=['m_row'], writes=['negm_row'])
            OP('vector', lambda Ee: Ee.tensor_tensor(m2_row, m_row, m_row,
                                                   ALU.mult),
               reads=['m_row'], writes=['m2_row'])
            OP('vector', lambda Ee: Ee.scalar_tensor_tensor(
                var_row, msq_row, float(EPS), m2_row, ALU.add, ALU.subtract),
               reads=['msq_row', 'm2_row'], writes=['var_row'])
            OP('scalar', lambda Ee: Ee.activation(sd_row, var_row, AF.Sqrt),
               reads=['var_row'], writes=['sd_row'])
            OP('vector', lambda Ee: Ee.reciprocal(rcp_row, sd_row),
               reads=['sd_row'], writes=['rcp_row'])
            OP('vector', lambda Ee, t=t: Ee.tensor_copy(
                r_all[0:1, t * 512:(t + 1) * 512], rcp_row),
               reads=['rcp_row'], writes=['r_all'])
            OP('vector', lambda Ee, t=t: Ee.tensor_tensor(
                mr_all[0:1, t * 512:(t + 1) * 512], m_row, rcp_row, ALU.mult),
               reads=['rcp_row', 'm_row'], writes=['mr_all'])
            OP('tensor', lambda Ee: Ee.matmul(pRAW[:], G_sb[0:1, :], negm_row,
                                            start=False, stop=True,
                                            skip_group_check=True),
               reads=['G', 'negm_row'], writes=['pRAW'])
            OP('vector', lambda Ee: Ee.tensor_copy(raw_sb[:], pRAW[:]),
               reads=['pRAW'], writes=['raw_sb'])
            OP('tensor', lambda Ee: Ee.matmul(pRB8[:], ones_r[0:1, 0:E],
                                            rcp_row, start=True, stop=True),
               reads=['rcp_row', 'ones_r'], writes=['pRB8'])
            OP('vector', lambda Ee: Ee.tensor_tensor(sraw_sb[:], raw_sb[:],
                                                   pRB8[:], ALU.mult),
               reads=['raw_sb', 'pRB8'], writes=['sraw'])
            OP('scalar', lambda Ee: Ee.activation(sig_sb[:], sraw_sb[:],
                                                AF.Sigmoid),
               reads=['sraw'], writes=['sig'])
            for q in range(4):
                OP('tensor', lambda Ee, q=q: Ee.matmul(
                    pTPa[:], sig_sb[0:E, q * 128:(q + 1) * 128],
                    identF[0:E, 0:E], start=True, stop=True),
                   reads=['sig', 'identF'], writes=['pTPa'])
                OP('vector', lambda Ee: Ee.tensor_copy(st_t[:, 0:E],
                                                     pTPa[:, 0:E]),
                   reads=['pTPa'], writes=['st'])
                OP('vector', lambda Ee: Ee.tensor_tensor(sb_t[:, 0:E],
                                                       st_t[:, 0:E],
                                                       ebias_t[:, 0:E],
                                                       ALU.add),
                   reads=['st', 'ebias_t'], writes=['sb'])
                OP('vector', lambda Ee: Ee.reduce_max(m1c[:], sb_t[:, 0:E],
                                                    axis=AX),
                   reads=['sb'], writes=['m1'], fence=True)
                OP('vector', lambda Ee: Ee.tensor_scalar(
                    eq_t[:, 0:E], sb_t[:, 0:E], m1c[:, 0:1], None, ALU.is_ge),
                   reads=['sb', 'm1'], writes=['eq'])
                OP('vector', lambda Ee: Ee.scalar_tensor_tensor(
                    sbx_t[:, 0:E], eq_t[:, 0:E], NEG, sb_t[:, 0:E],
                    ALU.mult, ALU.add),
                   reads=['eq', 'sb'], writes=['sbx'])
                OP('vector', lambda Ee: Ee.reduce_max(m2c[:], sbx_t[:, 0:E],
                                                    axis=AX),
                   reads=['sbx'], writes=['m2'], fence=True)
                OP('vector', lambda Ee: Ee.tensor_scalar(
                    top_t[:, 0:E], sb_t[:, 0:E], m2c[:, 0:1], None,
                    ALU.is_ge),
                   reads=['sb', 'm2'], writes=['top'])
                OP('vector', lambda Ee: Ee.tensor_tensor(w_t[:, 0:E],
                                                       st_t[:, 0:E],
                                                       top_t[:, 0:E],
                                                       ALU.mult),
                   reads=['st', 'top'], writes=['w'])
                OP('tensor', lambda Ee: Ee.matmul(pTPb[:], w_t[:, 0:E],
                                                identF[:, 0:128],
                                                start=True, stop=True),
                   reads=['w', 'identF'], writes=['pTPb'])
                OP('vector', lambda Ee, t=t, q=q: Ee.tensor_copy(
                    wT_sb[:, t * 512 + q * 128: t * 512 + (q + 1) * 128],
                    pTPb[0:E, :]),
                   reads=['pTPb'], writes=['wTs'])
        OP('sync', lambda Ee: Ee.dma_start(stats_d[0:1, :], r_all[:]),
           reads=['r_all'], writes=[('std',)], dma=True)
        OP('sync', lambda Ee: Ee.dma_start(stats_d[1:2, :], mr_all[:]),
           reads=['mr_all'], writes=[('std',)], dma=True)
        OP('sync', lambda Ee: Ee.dma_start(wT_d, wT_sb[:]),
           reads=['wTs'], writes=[('wtd',)], dma=True)
        sch.drain()
        sch.emit('P0')
        if phase_limit <= 0:
            nc.finalize()
            return nc

    with ExitStack() as _st2:
        xnT = _st2.enter_context(nc.sbuf_tensor('p1_xn', [128, DC * S], BF16))
        rws = _st2.enter_context(nc.sbuf_tensor('p1_rws', [1, 512 * 4], BF16))
        rBf = _st2.enter_context(nc.sbuf_tensor('p1_rB', [128, S], BF16))
        mrBf = _st2.enter_context(nc.sbuf_tensor('p1_mrB', [128, S], BF16))
        xb = _st2.enter_context(nc.sbuf_tensor('p1_xb', [128, 512 * 3], F32))
        tmpb = _st2.enter_context(nc.sbuf_tensor('p1_tmp', [128, 512 * 2], F32))
        wb = _st2.enter_context(nc.sbuf_tensor('p1_wb', [128, DC * 128 * 2], BF16))
        stg = _st2.enter_context(nc.sbuf_tensor('p1_st', [128, min(4, T8) * 512 * 2], BF16))
        vst = _st2.enter_context(nc.sbuf_tensor('p1_vst', [128, 512 * 2], BF16))
        vtt = _st2.enter_context(nc.sbuf_tensor('p1_vtt', [128, S * 2], BF16))
        pA0 = _st2.enter_context(nc.psum_tensor('p1_a0', [128, 512], F32))
        pA1 = _st2.enter_context(nc.psum_tensor('p1_a1', [128, 512], F32))
        pA2 = _st2.enter_context(nc.psum_tensor('p1_a2', [128, 512], F32))
        pA3 = _st2.enter_context(nc.psum_tensor('p1_a3', [128, 512], F32))
        pRB = _st2.enter_context(nc.psum_tensor('p1_rb', [128, 512], F32))
        pMRB = _st2.enter_context(nc.psum_tensor('p1_mrb', [128, 512], F32))
        pTR0 = _st2.enter_context(nc.psum_tensor('p1_tr0', [128, 128], BF16))
        pTR1 = _st2.enter_context(nc.psum_tensor('p1_tr1', [128, 128], BF16))

        pAcc = [pA0, pA1, pA2, pA3]
        pTRl = [pTR0, pTR1]

        def xns(d, c0, c1):
            return xnT[:, d * S + c0: d * S + c1]

        for t in range(T8):
            rr = rws[0:1, (t % 2) * 512:(t % 2 + 1) * 512]
            mm_ = rws[0:1, 1024 + (t % 2) * 512:1024 + (t % 2 + 1) * 512]
            OP('gpsimd', lambda Ee, rr=rr, t=t: Ee.dma_start(
                rr, stats_d[0:1, t * 512:(t + 1) * 512]),
               writes=[('rr', t % 2)], dma=True)
            OP('gpsimd', lambda Ee, mm_=mm_, t=t: Ee.dma_start(
                mm_, stats_d[1:2, t * 512:(t + 1) * 512]),
               writes=[('mr', t % 2)], dma=True)
            OP('tensor', lambda Ee, rr=rr: Ee.matmul(
                pRB[:], ones_rb[0:1, :], rr, start=True, stop=True),
               reads=[('rr', t % 2), 'ones_rb'], writes=['pRB'])
            OP('vector', lambda Ee, t=t: Ee.tensor_copy(
                rBf[:, t * 512:(t + 1) * 512], pRB[:]),
               reads=['pRB'], writes=[('rBf', t)])
            OP('tensor', lambda Ee, mm_=mm_: Ee.matmul(
                pMRB[:], ones_rb[0:1, :], mm_, start=True, stop=True),
               reads=[('mr', t % 2), 'ones_rb'], writes=['pMRB'])
            OP('vector', lambda Ee, t=t: Ee.tensor_copy(
                mrBf[:, t * 512:(t + 1) * 512], pMRB[:]),
               reads=['pMRB'], writes=[('mrBf', t)])
        for d in range(DC):
            for t in range(T8):
                n = d * T8 + t
                xs = xb[:, (n % 3) * 512:(n % 3 + 1) * 512]
                tm = tmpb[:, (n % 2) * 512:(n % 2 + 1) * 512]
                OP('sync', lambda Ee, xs=xs, d=d, t=t: Ee.dma_start(
                    xs, xT[d * 128:(d + 1) * 128, t * 512:(t + 1) * 512]),
                   writes=[('xb', n % 3)], dma=True)
                OP('vector', lambda Ee, xs=xs, tm=tm, t=t: Ee.tensor_tensor(
                    tm, xs, rBf[:, t * 512:(t + 1) * 512], ALU.mult),
                   reads=[('xb', n % 3), ('rBf', t)],
                   writes=[('tm', n % 2)])
                OP('vector', lambda Ee, tm=tm, d=d, t=t:
                   Ee.tensor_tensor(xns(d, t * 512, (t + 1) * 512), tm,
                                   mrBf[:, t * 512:(t + 1) * 512],
                                   ALU.subtract),
                   reads=[('tm', n % 2), ('mrBf', t)], writes=[('xn', d)])
            OP('sync', lambda Ee, d=d: Ee.dma_start(
                xn_d[d * 128:(d + 1) * 128, :], xns(d, 0, S)),
               reads=[('xn', d)], writes=[('xnd',)], dma=True)

        PROJ = [('q', wqT, qT_d), ('k', wkT, kT_d), ('v', wvT, None)]
        G4 = min(4, T8)
        GW = G4 * 512
        TGRP = T8 // G4
        widx = 0
        vpend = []

        def v_tail(dh_, t_, ti_):
            vs = vst[:, (t_ % 2) * 512:(t_ % 2 + 1) * 512]
            OP('vector', lambda Ee, vs=vs, ti_=ti_:
               Ee.tensor_copy(vs, pAcc[ti_][:]),
               reads=[('pA', ti_)], writes=[('vst', t_ % 2)])
            for sub in range(4):
                tok = t_ * 4 + sub
                pT = pTRl[sub % 2]
                OP('tensor', lambda Ee, vs=vs, sub=sub, pT=pT:
                   Ee.transpose(pT[:], vs[:, sub * 128:(sub + 1) * 128],
                                identB[:]),
                   reads=[('vst', t_ % 2), 'identB'],
                   writes=[('pTR', sub % 2)])
                OP('vector', lambda Ee, tok=tok, pT=pT, dh_=dh_:
                   Ee.tensor_copy(
                       vtt[:, (dh_ % 2) * S + tok * 128:
                           (dh_ % 2) * S + (tok + 1) * 128],
                       pT[:]),
                   reads=[('pTR', sub % 2)], writes=[('vtt', dh_ % 2)])
            if t_ == T8 - 1:
                OP('sync', lambda Ee, dh_=dh_: Ee.dma_start(
                    vtok_d[:, dh_ * 128:(dh_ + 1) * 128].rearrange(
                        '(a p) f -> p a f', p=128),
                    vtt[:, (dh_ % 2) * S:(dh_ % 2 + 1) * S].rearrange(
                        'p (a f) -> p a f', f=128)),
                   reads=[('vtt', dh_ % 2)], writes=[('vtokd',)], dma=True)

        for wname, wsrc, dst in PROJ:
            wsrc_r = wsrc.rearrange('(c p) f -> p c f', p=128)
            for dh in range(H):
                wi = widx % 2
                widx += 1
                wtile = wb[:, wi * DC * 128:(wi + 1) * DC * 128]
                OP('gpsimd', lambda Ee, wtile=wtile, wsrc_r=wsrc_r, dh=dh:
                   Ee.dma_start(wtile.rearrange('p (c f) -> p c f', c=DC),
                               wsrc_r[:, :, dh * 128:(dh + 1) * 128]),
                   writes=[('wb', wi)], dma=True)
                for tg in range(TGRP):
                    if wname != 'v':
                        for d in range(DC):
                            for ti in range(G4):
                                t = tg * G4 + ti
                                OP('tensor',
                                   lambda Ee, wtile=wtile, d=d, t=t, ti=ti:
                                   Ee.matmul(pAcc[ti][:],
                                            wtile[:, d * 128:(d + 1) * 128],
                                            xns(d, t * 512, (t + 1) * 512),
                                            start=(d == 0),
                                            stop=(d == DC - 1)),
                                   reads=[('wb', wi), ('xn', d)],
                                   writes=[('pA', ti)])
                        for ti in range(G4):
                            sg = stg[:, (tg % 2) * GW + ti * 512:
                                     (tg % 2) * GW + (ti + 1) * 512]
                            OP('vector', lambda Ee, sg=sg, ti=ti:
                               Ee.tensor_copy(sg, pAcc[ti][:]),
                               reads=[('pA', ti)], writes=[('stg', tg % 2)])
                        OP('sync', lambda Ee, dst=dst, dh=dh, tg=tg:
                           Ee.dma_start(
                               dst[dh * 128:(dh + 1) * 128,
                                   tg * GW:(tg + 1) * GW],
                               stg[:, (tg % 2) * GW:(tg % 2) * GW + GW]),
                           reads=[('stg', tg % 2)],
                           writes=[(wname + 'Td', tg)], dma=True)
                    else:
                        for ti in range(G4):
                            t = tg * G4 + ti
                            for d in range(DC):
                                OP('tensor',
                                   lambda Ee, wtile=wtile, d=d, t=t, ti=ti:
                                   Ee.matmul(pAcc[ti][:],
                                            wtile[:, d * 128:(d + 1) * 128],
                                            xns(d, t * 512, (t + 1) * 512),
                                            start=(d == 0),
                                            stop=(d == DC - 1)),
                                   reads=[('wb', wi), ('xn', d)],
                                   writes=[('pA', ti)])
                            if vpend:
                                v_tail(*vpend.pop(0))
                            vpend.append((dh, t, ti))
        while vpend:
            v_tail(*vpend.pop(0))
        qkeys = [('qTd', tg) for tg in range(TGRP)]
        xkeys = [('xnd',)]
        for s in range(NS):
            OP('sync', lambda Ee, s=s: Ee.dma_start(
                qTo_d.rearrange('(c p) f -> p c f', p=128)[
                    :, :, s * 512:(s + 1) * 512],
                qT_d.rearrange('(c p) f -> p c f', p=128)[
                    :, :, bass.ts(blk_sv[s], 512)]),
               reads=qkeys, writes=[('qTod',)], dma=True)
            OP('sync', lambda Ee, s=s: Ee.dma_start(
                xno_d.rearrange('(c p) f -> p c f', p=128)[
                    :, :, s * 512:(s + 1) * 512],
                xn_d.rearrange('(c p) f -> p c f', p=128)[
                    :, :, bass.ts(blk_sv[s], 512)]),
               reads=xkeys, writes=[('xnod',)], dma=True)
            OP('sync', lambda Ee, s=s: Ee.dma_start(
                wTo_d[:, s * 512:(s + 1) * 512],
                wT_d[:, bass.ts(blk_sv[s], 512)]),
               writes=[('wTod',)], dma=True)
        sch.drain()
        sch.emit('P1')
        if phase_limit <= 2:
            nc.finalize()
            return nc

    KTmax = max(cfg.KT)
    with ExitStack() as _st3:
        alk = _st3.enter_context(nc.sbuf_tensor('p2_alk', [128, H * 128], F32))
        slp = _st3.enter_context(nc.sbuf_tensor('p2_slp', [1, H * 128], BF16))
        qr = _st3.enter_context(nc.sbuf_tensor('p2_qr', [1, NSB], BF16))
        can = _st3.enter_context(nc.sbuf_tensor('p2_can', [128, 4 * 512], F32))
        flg = _st3.enter_context(nc.sbuf_tensor('p2_flg', [128, 128], F32))
        kb = _st3.enter_context(nc.sbuf_tensor('p2_kb', [128, KTmax * 128 * 3], BF16))
        vb = _st3.enter_context(nc.sbuf_tensor('p2_vb', [128, KTmax * 128 * 3], BF16))
        pb = _st3.enter_context(nc.sbuf_tensor('p2_pb', [128, 512 * 4], BF16))
        scb = _st3.enter_context(nc.sbuf_tensor('p2_scb', [128, 512 * 4], F32))
        dnrow = _st3.enter_context(nc.sbuf_tensor('p2_dn', [1, 512 * 2], F32))
        rcrow = _st3.enter_context(nc.sbuf_tensor('p2_rc', [1, 512 * 2], BF16))
        rbsb = _st3.enter_context(nc.sbuf_tensor('p2_rbs', [128, 512 * 2], F32))
        pBsb = _st3.enter_context(nc.sbuf_tensor('p2_pBs', [128, 512 * 2], BF16))
        zsb = _st3.enter_context(nc.sbuf_tensor('p2_zsb', [128, H * 512 * 2], BF16))
        qsbuf = _st3.enter_context(nc.sbuf_tensor('p2_qs', [128, H * 512 * 2], BF16))
        wob = _st3.enter_context(nc.sbuf_tensor('p2_wo', [128, H * 128 * 2], BF16))
        ast = _st3.enter_context(nc.sbuf_tensor('p2_ast', [128, 512 * 2], BF16))
        pS0 = _st3.enter_context(nc.psum_tensor('p2_s0', [128, 512], F32))
        pS1 = _st3.enter_context(nc.psum_tensor('p2_s1', [128, 512], F32))
        pS2 = _st3.enter_context(nc.psum_tensor('p2_s2', [128, 512], F32))
        pZ0 = _st3.enter_context(nc.psum_tensor('p2_z0', [128, 512], F32))
        pZ1 = _st3.enter_context(nc.psum_tensor('p2_z1', [128, 512], F32))
        pDN = _st3.enter_context(nc.psum_tensor('p2_dnp', [1, 512], F32))
        pWO = _st3.enter_context(nc.psum_tensor('p2_wop', [128, 512], F32))
        pRB2 = _st3.enter_context(nc.psum_tensor('p2_rb2', [128, 512], F32))

        pS = [pS0, pS1, pS2]
        pZl = [pZ0, pZ1]
        wo_r = woT.rearrange('(c p) f -> p c f', p=128)

        def wo_block(sb, dm):
            wt = wob[:, (dm % 2) * H * 128:(dm % 2 + 1) * H * 128]
            OP('gpsimd', lambda Ee, wt=wt, dm=dm: Ee.dma_start(
                wt.rearrange('p (c f) -> p c f', c=H),
                wo_r[:, :, dm * 128:(dm + 1) * 128]),
               writes=[('wo', dm % 2)], dma=True)
            for hh in range(H):
                OP('tensor', lambda Ee, wt=wt, hh=hh, sb=sb:
                   Ee.matmul(pWO[:], wt[:, hh * 128:(hh + 1) * 128],
                            zsb[:, (sb % 2) * H * 512 + hh * 512:
                                (sb % 2) * H * 512 + (hh + 1) * 512],
                            start=(hh == 0), stop=(hh == H - 1)),
                   reads=[('wo', dm % 2), ('zsb', sb % 2, hh)],
                   writes=['pWO'])
            at = ast[:, (dm % 2) * 512:(dm % 2 + 1) * 512]
            OP('vector', lambda Ee, at=at: Ee.tensor_copy(at, pWO[:]),
               reads=['pWO'], writes=[('astw', dm % 2)])
            OP('sync', lambda Ee, at=at, dm=dm, sb=sb: Ee.dma_start(
                attn_d[dm * 128:(dm + 1) * 128, sb * 512:(sb + 1) * 512], at),
               reads=[('astw', dm % 2)], writes=[('attnd',)], dma=True)

        OP('sync', lambda Ee: Ee.dma_start(alk[:], alibik_i), writes=['alk'],
           dma=True)
        OP('sync', lambda Ee: Ee.dma_start(slp[:], slopes_i), writes=['slp'],
           dma=True)
        OP('sync', lambda Ee: Ee.dma_start(qr[:], qrow_i), writes=['qr'],
           dma=True)
        OP('sync', lambda Ee: Ee.dma_start(can[:], canon_i), writes=['can'],
           dma=True)
        OP('sync', lambda Ee: Ee.dma_start(flg[:], flags_i), writes=['flg'],
           dma=True)

        apend = []
        for s in range(NS):
            KT = cfg.KT[s]
            qsb = qsbuf[:, (s % 2) * H * 512:(s % 2) * H * 512 + H * 512]
            OP('sync', lambda Ee, qsb=qsb, s=s: Ee.dma_start(
                qsb.rearrange('p (c f) -> p c f', c=H),
                qTo_d.rearrange('(c p) f -> p c f', p=128)[
                    :, :, s * 512:(s + 1) * 512]),
               writes=[('qs', s % 2)], dma=True)
            for h in range(H):
                u = s * H + h
                kbu = kb[:, (u % 3) * KTmax * 128:
                         (u % 3) * KTmax * 128 + KT * 128]
                vbu = vb[:, (u % 3) * KTmax * 128:
                         (u % 3) * KTmax * 128 + KT * 128]
                qbu = qsb[:, h * 512:(h + 1) * 512]
                OP('sync', lambda Ee, kbu=kbu, h=h, KT=KT: Ee.dma_start(
                    kbu, kT_d[h * 128:(h + 1) * 128, 0:KT * 128]),
                   writes=[('kb', u % 3)], dma=True)
                OP('gpsimd', lambda Ee, vbu=vbu, h=h, KT=KT: Ee.dma_start(
                    vbu.rearrange('p (a f) -> p a f', f=128),
                    vtok_d[0:KT * 128, h * 128:(h + 1) * 128].rearrange(
                        '(a p) f -> p a f', p=128)),
                   writes=[('vb', u % 3)], dma=True)
                pZ = pZl[u % 2]
                cand = (cfg.own[0][s], cfg.own[1][s])
                pBu = pBsb[:, (u % 2) * 512:(u % 2 + 1) * 512]

                OP('tensor', lambda Ee, h=h, s=s: Ee.matmul(
                    pRB2[:], slp[0:1, h * 128:(h + 1) * 128],
                    qr[0:1, s * 512:(s + 1) * 512], start=True, stop=True),
                   reads=['slp', 'qr'], writes=['pRB2'])
                OP('vector', lambda Ee, pBu=pBu: Ee.tensor_copy(pBu, pRB2[:]),
                   reads=['pRB2'], writes=[('pBs', u % 2)])

                def d_qk(kc):
                    n3 = kc % 3
                    sb_ = pS[n3]
                    OP('tensor', lambda Ee, sb_=sb_, kbu=kbu, qbu=qbu, kc=kc:
                       Ee.matmul(sb_[:], kbu[:, kc * 128:(kc + 1) * 128], qbu,
                                start=True, stop=True),
                       reads=[('kb', u % 3), ('qs', s % 2)],
                       writes=[('pS', n3)])

                def d_clip(kc):
                    n3 = kc % 3
                    n4 = kc % 4
                    sb_ = pS[n3]
                    scs = scb[:, n4 * 512:(n4 + 1) * 512]
                    OP('vector', lambda Ee, sb_=sb_, scs=scs, pBu=pBu:
                       Ee.scalar_tensor_tensor(
                           scs, sb_[:], CLIP, pBu, ALU.min, ALU.add),
                       reads=[('pS', n3), ('pBs', u % 2)],
                       writes=[('sc', n4)])
                    if kc // 4 in cand:
                        OP('vector', lambda Ee, scs=scs, s=s, kc=kc:
                           Ee.scalar_tensor_tensor(
                               scs,
                               can[:, (kc % 4) * 512:(kc % 4 + 1) * 512],
                               flg[:, s * 32 + kc: s * 32 + kc + 1], scs,
                               ALU.mult, ALU.add),
                           reads=[('sc', n4), 'can', 'flg'],
                           writes=[('sc', n4)])

                def d_exp(kc):
                    n4 = kc % 4
                    scs = scb[:, n4 * 512:(n4 + 1) * 512]
                    pbu = pb[:, n4 * 512:(n4 + 1) * 512]
                    OP('scalar', lambda Ee, scs=scs, pbu=pbu, h=h, s=s, kc=kc:
                       Ee.activation(pbu, scs, AF.Exp,
                                    bias=alk[:, h * 128 + s * 32 + kc:
                                             h * 128 + s * 32 + kc + 1],
                                    scale=SC),
                       reads=[('sc', n4), 'alk'], writes=[('pb', n4)])

                def d_dnav(kc):
                    n4 = kc % 4
                    pbu = pb[:, n4 * 512:(n4 + 1) * 512]
                    OP('tensor', lambda Ee, pbu=pbu, kc=kc, KT=KT:
                       Ee.matmul(
                        pDN[:], ones_cb[:], pbu,
                        start=(kc == 0), stop=(kc == KT - 1)),
                       reads=[('pb', n4), 'ones_cb'],
                       writes=[('pDN',)])
                    OP('tensor', lambda Ee, pZ=pZ, pbu=pbu, vbu=vbu, kc=kc,
                       KT=KT:
                       Ee.matmul(pZ[:], vbu[:, kc * 128:(kc + 1) * 128], pbu,
                                start=(kc == 0), stop=(kc == KT - 1)),
                       reads=[('pb', n4), ('vb', u % 3)],
                       writes=[('pZ', u % 2)])

                for i in range(KT + 6):
                    if i == 8 and apend:
                        apend.pop(0)()
                    if i < KT:
                        d_qk(i)
                    if 2 <= i < KT + 2:
                        d_clip(i - 2)
                    if 4 <= i < KT + 4:
                        d_exp(i - 4)
                    if i >= 6:
                        d_dnav(i - 6)
                dnr = dnrow[0:1, (u % 2) * 512:(u % 2 + 1) * 512]
                rcr = rcrow[0:1, (u % 2) * 512:(u % 2 + 1) * 512]
                rbs = rbsb[:, (u % 2) * 512:(u % 2 + 1) * 512]
                OP('vector', lambda Ee, dnr=dnr:
                   Ee.tensor_copy(dnr, pDN[:]),
                   reads=[('pDN',)], writes=[('dnr', u % 2)])

                def unit_tail(u=u, s=s, h=h, pZ=pZ, dnr=dnr, rcr=rcr,
                              rbs=rbs):
                    def _recip(Ee, dnr=dnr, rcr=rcr):
                        with nc.allow_low_precision(
                                reason='softmax 1/DN bf16'):
                            return Ee.reciprocal(rcr, dnr)
                    OP('vector', _recip,
                       reads=[('dnr', u % 2)], writes=[('rcr', u % 2)])
                    OP('tensor', lambda Ee, rcr=rcr: Ee.matmul(
                        pRB2[:], ones_rb[0:1, :], rcr, start=True, stop=True),
                       reads=[('rcr', u % 2), 'ones_rb'], writes=['pRB2'])
                    OP('vector', lambda Ee, rbs=rbs:
                       Ee.tensor_copy(rbs, pRB2[:]),
                       reads=['pRB2'], writes=[('rbs', u % 2)])
                    OP('vector', lambda Ee, pZ=pZ, rbs=rbs, s=s, h=h:
                       Ee.tensor_tensor(
                           zsb[:, (s % 2) * H * 512 + h * 512:
                               (s % 2) * H * 512 + (h + 1) * 512],
                           pZ[:], rbs, ALU.mult),
                       reads=[('pZ', u % 2), ('rbs', u % 2)],
                       writes=[('zsb', s % 2, h)])
                apend.append(unit_tail)
                if s >= 1:
                    wo_block(s - 1, h)
        while apend:
            apend.pop(0)()
        for dm in range(DC):
            wo_block(NS - 1, dm)
        sch.drain()
        sch.emit('P2')
        if phase_limit <= 3:
            nc.finalize()
            return nc

    NT = NS
    with ExitStack() as _st5:
        xno = _st5.enter_context(nc.sbuf_tensor('p5_xn', [128, DC * NSB], BF16))
        acc = _st5.enter_context(nc.sbuf_tensor('p5_acc', [128, DC * NSB], BF16))

        def xos(d):
            return xno[:, d * NSB:(d + 1) * NSB]

        with ExitStack() as _st6:
            atb = _st6.enter_context(nc.sbuf_tensor('p5_at', [128, 512 * 2], BF16))
            xob = _st6.enter_context(nc.sbuf_tensor('p5_xo', [128, 512 * 2], F32))
            hx = _st6.enter_context(nc.sbuf_tensor('p5_hx', [128, HC * NSB], BF16))
            w13 = _st6.enter_context(nc.sbuf_tensor('p5_w13', [128, DC * 128 * 4], BF16))
            wrow = _st6.enter_context(nc.sbuf_tensor('p5_wrow', [1, NSB], BF16))
            w2b = _st6.enter_context(nc.sbuf_tensor('p5_w2', [128, HC * 128 * 2], BF16))
            wB = _st6.enter_context(nc.sbuf_tensor('p5_wB', [128, NSB], F32))
            gt = _st6.enter_context(nc.sbuf_tensor('p5_gt', [128, 512 * 2], F32))
            slb = _st6.enter_context(nc.sbuf_tensor('p5_sl', [128, 512 * 2], F32))
            gP0 = _st6.enter_context(nc.psum_tensor('p5_g0', [128, 512], F32))
            gP1 = _st6.enter_context(nc.psum_tensor('p5_g1', [128, 512], F32))
            gP2 = _st6.enter_context(nc.psum_tensor('p5_g2', [128, 512], F32))
            gP3 = _st6.enter_context(nc.psum_tensor('p5_g3', [128, 512], F32))
            uP0 = _st6.enter_context(nc.psum_tensor('p5_u0', [128, 512], F32))
            uP1 = _st6.enter_context(nc.psum_tensor('p5_u1', [128, 512], F32))
            uP2 = _st6.enter_context(nc.psum_tensor('p5_u2', [128, 512], F32))
            uP3 = _st6.enter_context(nc.psum_tensor('p5_u3', [128, 512], F32))

            gP = [gP0, gP1, gP2, gP3]
            uP = [uP0, uP1, uP2, uP3]

            OP('sync', lambda Ee: Ee.dma_start(
                xno.rearrange('p (c f) -> p c f', c=DC),
                xno_d.rearrange('(c p) f -> p c f', p=128)),
               writes=[('xno', d) for d in range(DC)], dma=True)

            for e in range(E + 1):
                shared = (e == E)
                if not shared:
                    OP('gpsimd', lambda Ee, e=e: Ee.dma_start(
                        wrow[0:1, :], wTo_d[e:e + 1, :]),
                       writes=['wrow'], dma=True)
                    for t in range(NT):
                        OP('tensor', lambda E_, t=t: E_.matmul(
                            gP0[:], ones_rb[0:1, :],
                            wrow[0:1, t * 512:(t + 1) * 512],
                            start=True, stop=True),
                           reads=['wrow', 'ones_rb'], writes=[('gP', 0)])
                        OP('vector', lambda E_, t=t: E_.tensor_copy(
                            wB[:, t * 512:(t + 1) * 512], gP0[:]),
                           reads=[('gP', 0)], writes=['wB'])
                for hc in range(HC):
                    wi = (e * HC + hc) % 2
                    wt1 = w13[:, wi * DC * 128:(wi + 1) * DC * 128]
                    wt3 = w13[:, (2 + wi) * DC * 128:(3 + wi) * DC * 128]
                    src1 = ((sw1T if shared else None) or None)
                    if shared:
                        s1r = sw1T.rearrange('(c p) f -> p c f', p=128)
                        s3r = sw3T.rearrange('(c p) f -> p c f', p=128)
                    else:
                        s1r = w1T[e].rearrange('(c p) f -> p c f', p=128)
                        s3r = w3T[e].rearrange('(c p) f -> p c f', p=128)
                    OP('gpsimd', lambda E_, wt1=wt1, s1r=s1r, hc=hc:
                       E_.dma_start(wt1.rearrange('p (c f) -> p c f', c=DC),
                                    s1r[:, :, hc * 128:(hc + 1) * 128]),
                       writes=[('w13a', wi)], dma=True)
                    OP('gpsimd', lambda E_, wt3=wt3, s3r=s3r, hc=hc:
                       E_.dma_start(wt3.rearrange('p (c f) -> p c f', c=DC),
                                    s3r[:, :, hc * 128:(hc + 1) * 128]),
                       writes=[('w13b', wi)], dma=True)
                    for d in range(DC):
                        for t in range(NT):
                            OP('tensor', lambda E_, wt1=wt1, d=d, t=t:
                               E_.matmul(gP[t][:],
                                         wt1[:, d * 128:(d + 1) * 128],
                                         xos(d)[:, t * 512:(t + 1) * 512],
                                         start=(d == 0), stop=(d == DC - 1)),
                               reads=[('w13a', wi), ('xno', d)],
                               writes=[('gP', t)])
                        for t in range(NT):
                            OP('tensor', lambda E_, wt3=wt3, d=d, t=t:
                               E_.matmul(uP[t][:],
                                         wt3[:, d * 128:(d + 1) * 128],
                                         xos(d)[:, t * 512:(t + 1) * 512],
                                         start=(d == 0), stop=(d == DC - 1)),
                               reads=[('w13b', wi), ('xno', d)],
                               writes=[('uP', t)])
                    for t in range(NT):
                        gts = gt[:, (t % 2) * 512:(t % 2 + 1) * 512]
                        sls = slb[:, (t % 2) * 512:(t % 2 + 1) * 512]
                        if shared:
                            OP('scalar', lambda E_, sls=sls, t=t:
                               E_.activation(sls, gP[t][:], AF.Silu),
                               reads=[('gP', t)], writes=[('sl', t % 2)])
                        else:
                            OP('vector', lambda E_, gts=gts, t=t:
                               E_.tensor_tensor(gts, gP[t][:],
                                                wB[:, t * 512:(t + 1) * 512],
                                                ALU.mult),
                               reads=[('gP', t), 'wB'], writes=[('gt', t % 2)])
                            OP('scalar', lambda E_, sls=sls, gts=gts:
                               E_.activation(sls, gts, AF.Silu),
                               reads=[('gt', t % 2)], writes=[('sl', t % 2)])
                        OP('vector', lambda E_, sls=sls, hc=hc, t=t:
                           E_.tensor_tensor(
                               hx[:, hc * NSB + t * 512:
                                  hc * NSB + (t + 1) * 512],
                               sls, uP[t][:], ALU.mult),
                           reads=[('sl', t % 2), ('uP', t)],
                           writes=[('hx', hc)])
                    if not shared:
                        OP('vector', lambda E_, hc=hc: E_.tensor_tensor(
                            hx[:, hc * NSB:(hc + 1) * NSB],
                            hx[:, hc * NSB:(hc + 1) * NSB],
                            wB[:], ALU.mult),
                           reads=[('hx', hc), 'wB'], writes=[('hx', hc)])
                for dm in range(DC):
                    wi = (e * DC + dm) % 2
                    wt2 = w2b[:, wi * HC * 128:(wi + 1) * HC * 128]
                    if shared:
                        s2r = sw2T.rearrange('(c p) f -> p c f', p=128)
                    else:
                        s2r = w2T[e].rearrange('(c p) f -> p c f', p=128)
                    OP('gpsimd', lambda E_, wt2=wt2, s2r=s2r, dm=dm:
                       E_.dma_start(wt2.rearrange('p (c f) -> p c f', c=HC),
                                    s2r[:, :, dm * 128:(dm + 1) * 128]),
                       writes=[('w2b', wi)], dma=True)
                    for t in range(NT):
                        if shared:
                            n = dm * NT + t
                            at = atb[:, (n % 2) * 512:(n % 2 + 1) * 512]
                            xo = xob[:, (n % 2) * 512:(n % 2 + 1) * 512]
                            OP('gpsimd', lambda Ee, at=at, dm=dm, t=t:
                               Ee.dma_start(
                                   at, attn_d[dm * 128:(dm + 1) * 128,
                                              t * 512:(t + 1) * 512]),
                               writes=[('at', n % 2)], dma=True)
                            OP('gpsimd', lambda Ee, xo=xo, dm=dm, t=t:
                               Ee.dma_start(
                                   xo, xTo[dm * 128:(dm + 1) * 128,
                                           t * 512:(t + 1) * 512]),
                               writes=[('xo', n % 2)], dma=True)
                        for hc in range(HC):
                            OP('tensor', lambda E_, wt2=wt2, hc=hc, t=t:
                               E_.matmul(gP[t][:],
                                         wt2[:, hc * 128:(hc + 1) * 128],
                                         hx[:, hc * NSB + t * 512:
                                            hc * NSB + (t + 1) * 512],
                                         start=(hc == 0),
                                         stop=(hc == HC - 1)),
                               reads=[('w2b', wi), ('hx', hc)],
                               writes=[('gP', t)])
                        aslice = acc[:, dm * NSB + t * 512:
                                     dm * NSB + (t + 1) * 512]
                        if e == 0:
                            OP('vector', lambda E_, aslice=aslice, t=t:
                               E_.tensor_copy(aslice, gP[t][:]),
                               reads=[('gP', t)], writes=[('acc', dm)])
                        elif not shared:
                            OP('vector', lambda E_, aslice=aslice, t=t:
                               E_.tensor_tensor(aslice, aslice, gP[t][:],
                                                ALU.add),
                               reads=[('gP', t), ('acc', dm)],
                               writes=[('acc', dm)])
                        else:
                            n = dm * NT + t
                            at = atb[:, (n % 2) * 512:(n % 2 + 1) * 512]
                            xo = xob[:, (n % 2) * 512:(n % 2 + 1) * 512]
                            o12 = gt[:, (n % 2) * 512:(n % 2 + 1) * 512]
                            OP('vector', lambda E_, o12=o12, aslice=aslice,
                               at=at:
                               E_.tensor_tensor(o12, aslice, at, ALU.add),
                               reads=[('at', n % 2), ('acc', dm)],
                               writes=[('o12', n % 2), ('gt', n % 2)])
                            OP('vector', lambda E_, o12=o12, xo=xo:
                               E_.tensor_tensor(o12, o12, xo, ALU.add),
                               reads=[('xo', n % 2), ('o12', n % 2)],
                               writes=[('o12', n % 2)])
                            OP('vector', lambda E_, o12=o12, t=t:
                               E_.tensor_tensor(o12, o12, gP[t][:], ALU.add),
                               reads=[('gP', t), ('o12', n % 2)],
                               writes=[('o12', n % 2)])
                            OP('sync', lambda Ee, o12=o12, dm=dm, t=t:
                               Ee.dma_start(
                                   out_t[dm * 128:(dm + 1) * 128,
                                         t * 512:(t + 1) * 512], o12),
                               reads=[('o12', n % 2)], writes=[('outw',)],
                               dma=True)
            sch.drain()
            sch.emit('P5a')

    nc.finalize()
    return nc



_CACHE = {}


def _get_nc(cfg):
    key = (cfg.D, cfg.S, cfg.H, cfg.HID, cfg.E)
    if key not in _CACHE:
        _CACHE[key] = build_nc(cfg)
    return _CACHE[key]


def make_core_inputs(cfg, c, x, Wq, Wk, Wv, Wo, gate, expert_bias,
                     w1, w2, w3, shw1, shw2, shw3):
    b, p = c // 2, c % 2
    cst = make_core_consts(cfg, p)
    xb = np.asarray(x[b], dtype=np.float32)
    own = cfg.own[p]
    x_own = np.concatenate([xb[g * BLK:(g + 1) * BLK] for g in own], axis=0)
    d = dict(
        xT=np.ascontiguousarray(xb.T),
        xTo=np.ascontiguousarray(x_own.T),
        wqT=np.ascontiguousarray(np.asarray(Wq, np.float32).T),
        wkT=np.ascontiguousarray(np.asarray(Wk, np.float32).T),
        wvT=np.ascontiguousarray(np.asarray(Wv, np.float32).T),
        woT=np.ascontiguousarray(np.asarray(Wo, np.float32).T),
        gateT=np.ascontiguousarray(np.asarray(gate, np.float32).T),
        ebias=np.ascontiguousarray(np.asarray(expert_bias,
                                              np.float32)[None, :]),
        w1T=np.ascontiguousarray(np.transpose(np.asarray(w1, np.float32),
                                              (0, 2, 1))),
        w3T=np.ascontiguousarray(np.transpose(np.asarray(w3, np.float32),
                                              (0, 2, 1))),
        w2T=np.ascontiguousarray(np.transpose(np.asarray(w2, np.float32),
                                              (0, 2, 1))),
        sw1T=np.ascontiguousarray(np.asarray(shw1, np.float32).T),
        sw3T=np.ascontiguousarray(np.asarray(shw3, np.float32).T),
        sw2T=np.ascontiguousarray(np.asarray(shw2, np.float32).T),
    )
    d.update(cst)
    return d


def kernel(L0_attn_resid_pre, L0_attn_Wq, L0_attn_Wk, L0_attn_Wv, L0_attn_Wo,
           L0_ffn_gate, L0_ffn_expert_bias, L0_ffn_w1, L0_ffn_w2, L0_ffn_w3,
           L0_ffn_shared_w1, L0_ffn_shared_w2, L0_ffn_shared_w3):
    cfg = Cfg()
    x = np.asarray(L0_attn_resid_pre, dtype=np.float32)
    nc = _get_nc(cfg)
    in_maps = [make_core_inputs(cfg, c, x, L0_attn_Wq, L0_attn_Wk, L0_attn_Wv,
                                L0_attn_Wo, L0_ffn_gate, L0_ffn_expert_bias,
                                L0_ffn_w1, L0_ffn_w2, L0_ffn_w3,
                                L0_ffn_shared_w1, L0_ffn_shared_w2,
                                L0_ffn_shared_w3)
               for c in range(8)]
    res = run_bass_kernel_spmd(nc, in_maps, core_ids=list(range(8)))
    out = np.empty((cfg.B, cfg.S, cfg.D), dtype=np.float32)
    for c in range(8):
        b, p = c // 2, c % 2
        ot = res.results[c]['out_t']
        for s, g in enumerate(cfg.own[p]):
            out[b, g * BLK:(g + 1) * BLK, :] = \
                np.asarray(ot[:, s * BLK:(s + 1) * BLK]).T
    return out


# version 15 (final)
# speedup vs baseline: 1.0278x; 1.0278x over previous
import math
import numpy as np
import ml_dtypes
from contextlib import ExitStack
import concourse.bass as bass
import concourse.mybir as mybir
from concourse.bass_utils import run_bass_kernel_spmd

F32 = mybir.dt.float32
BF16 = mybir.dt.bfloat16
U32 = mybir.dt.uint32
ALU = mybir.AluOpType
AF = mybir.ActivationFunctionType
AX = mybir.AxisListType.X
bfdt = ml_dtypes.bfloat16

BLK = 512
EPS = 1e-5
CSHIFT = 24.0
SC = 1.0 / math.sqrt(128.0)
CLIP = 1024.0 * math.sqrt(128.0)
NEG = -1e30

ENGINES = ('sync', 'gpsimd', 'tensor', 'vector', 'scalar')


class Cfg:
    def __init__(self, D=2048, S=4096, H=16, HID=1024, E=8, B=4):
        self.D, self.S, self.H, self.HID, self.E, self.B = D, S, H, HID, E, B
        self.DC = D // 128
        self.HC = HID // 128
        self.NB = S // BLK
        self.NS = self.NB // 2
        self.OWN = self.NS * BLK
        self.T8 = S // 512
        pat = ([0, 1, 1, 0] * ((self.NB + 3) // 4))[:self.NB]
        self.own = {0: [i for i, v in enumerate(pat) if v == 0],
                    1: [i for i, v in enumerate(pat) if v == 1]}
        assert len(self.own[0]) == self.NS and len(self.own[1]) == self.NS
        self.KT = [4 * (max(self.own[0][s], self.own[1][s]) + 1)
                   for s in range(self.NS)]


def _slopes(n):
    def pow2(m):
        start = 2 ** (-2 ** (-(math.log2(m) - 3)))
        return [start * start ** i for i in range(m)]
    if math.log2(n).is_integer():
        return pow2(n)
    c = 2 ** math.floor(math.log2(n))
    return pow2(c) + _slopes(2 * c)[0::2][:n - c]


def make_core_consts(cfg, parity):
    H, NS = cfg.H, cfg.NS
    sl = np.array(_slopes(H), dtype=np.float64)
    own = cfg.own[parity]
    slopes_rep = np.repeat(sl, 128).astype(bfdt)[None, :]
    qrow = np.zeros((1, NS * BLK), np.float64)
    for s, g in enumerate(own):
        q = g * BLK + np.arange(BLK)
        qrow[0, s * BLK:(s + 1) * BLK] = -math.sqrt(128.0) * q
    qrow = qrow.astype(bfdt)
    alibi_k = np.full((128, H * 128), NEG, np.float32)
    for h in range(H):
        for s in range(NS):
            kmax_t = 4 * (own[s] + 1)
            for kc in range(cfg.KT[s]):
                col = h * 128 + s * 32 + kc
                if kc < kmax_t:
                    j = kc * 128 + np.arange(128)
                    alibi_k[:, col] = (sl[h] * j - CSHIFT).astype(np.float32)
    flags = np.ones((128, 128), np.float32)
    for s in range(NS):
        for kc in range(cfg.KT[s]):
            if kc // 4 == own[s]:
                flags[:, s * 32 + kc] = 0.0
    canon = np.zeros((128, 4 * 512), np.float32)
    for o in range(4):
        for sub in range(4):
            blkv = canon[:, o * 512 + sub * 128: o * 512 + (sub + 1) * 128]
            if sub < o:
                blkv[:] = NEG
            elif sub == o:
                jj = np.arange(128)[:, None]
                qq = np.arange(128)[None, :]
                blkv[:] = np.where(qq >= jj, 0.0, NEG).astype(np.float32)
    flags = 1.0 - flags
    return dict(slopes_rep=slopes_rep, qrow=qrow, alibi_k=alibi_k,
                flags=flags, canon=canon,
                blkidx=np.array([own], dtype=np.uint32),
                identB=np.eye(128, dtype=bfdt),
                identF=np.eye(128, dtype=np.float32))


class Op:
    __slots__ = ('oid', 'eng', 'fn', 'deps', 'dma', 'fence',
                 'inc_val', 'tag', 'semkey', 'sem')

    def __init__(self, oid, eng, fn, deps, dma, fence, tag):
        self.oid, self.eng, self.fn = oid, eng, fn
        self.deps, self.dma, self.fence = deps, dma, fence
        self.inc_val = None
        self.tag = tag


class Sched:

    def __init__(self, nc):
        self.nc = nc
        self.csem = {e: nc.alloc_semaphore(f'cs_{e}') for e in ENGINES}
        self.ccnt = {e: 0 for e in ENGINES}
        self.dsems = {}
        self._reset()

    def _reset(self):
        self.ops = []
        self.last_w = {}
        self.readers = {}

    def _dsem(self, key):
        if key not in self.dsems:
            self.dsems[key] = [self.nc.alloc_semaphore(f'dk{len(self.dsems)}'),
                               0]
        return self.dsems[key]

    def op(self, eng, fn, reads=(), writes=(), dma=False, fence=False,
           deps=(), tag=''):
        dep = set(deps)
        for k in reads:
            if k in self.last_w:
                dep.add(self.last_w[k])
        for k in writes:
            for r in self.readers.get(k, ()):
                dep.add(r)
            if k in self.last_w:
                p = self.ops[self.last_w[k]]
                if not (dma and p.dma):
                    dep.add(self.last_w[k])
        oid = len(self.ops)
        o = Op(oid, eng, fn, sorted(dep), dma, fence, tag)
        if dma:
            o.semkey = writes[0] if writes else ('uniq', oid)
        self.ops.append(o)
        for k in reads:
            self.readers.setdefault(k, []).append(oid)
        for k in writes:
            self.last_w[k] = oid
            self.readers[k] = []
        return oid

    def drain(self):
        dmas = tuple(o.oid for o in self.ops if o.dma)
        if dmas:
            self.op('sync', lambda E: E.nop(), deps=dmas, tag='drain')
            self.op('gpsimd', lambda E: E.nop(), deps=dmas, tag='drain')

    def emit(self, name):
        ops = self.ops
        needed = set()
        for o in ops:
            for d in o.deps:
                p = ops[d]
                if not (p.eng == o.eng == 'tensor' and not p.fence
                        and not p.dma):
                    needed.add(d)
        for o in ops:
            if o.dma:
                ds = self._dsem(o.semkey)
                ds[1] += 16
                o.inc_val = ds[1]
                o.sem = ds[0]
            elif o.oid in needed:
                self.ccnt[o.eng] += 1
                o.inc_val = self.ccnt[o.eng]
                o.sem = self.csem[o.eng]
        per_eng = {e: [o for o in ops if o.eng == e] for e in ENGINES}
        nc = self.nc
        with nc.Block(name) as blk:
            for eng in ENGINES:
                eops = per_eng[eng]
                if not eops:
                    continue

                def body(E, eops=eops, eng=eng):
                    hw = {}
                    for o in eops:
                        want = {}
                        for d in o.deps:
                            p = ops[d]
                            if p.inc_val is None:
                                continue
                            if (p.eng == eng == 'tensor'
                                    and not (p.fence or p.dma)):
                                continue
                            cur = want.get(p.sem.name)
                            if cur is None or cur[1] < p.inc_val:
                                want[p.sem.name] = (p.sem, p.inc_val)
                        for name, (sem, val) in want.items():
                            if hw.get(name, -1) < val:
                                E.wait_ge(sem, val)
                                hw[name] = val
                        inst = o.fn(E)
                        if o.inc_val is not None:
                            inst.then_inc(o.sem, 16 if o.dma else 1)
                getattr(blk, eng)(body)
        self._reset()



def build_nc(cfg, phase_limit=99, dbg=False):
    nc = bass.Bass('TRN2', target_bir_lowering=False,
                   detect_race_conditions=False)
    D, S, H, HID, E = cfg.D, cfg.S, cfg.H, cfg.HID, cfg.E
    DC, HC, NS, OWN, T8 = cfg.DC, cfg.HC, cfg.NS, cfg.OWN, cfg.T8
    NSB = NS * BLK

    def din(name, shape, dt=F32):
        return nc.dram_tensor(name, list(shape), dt, kind='ExternalInput').ap()

    xT = din('xT', [D, S])
    xTo = din('xTo', [D, OWN])
    wqT = din('wqT', [D, H * 128])
    wkT = din('wkT', [D, H * 128])
    wvT = din('wvT', [D, H * 128])
    woT = din('woT', [H * 128, D])
    gateT = din('gateT', [D, E])
    ebias = din('ebias', [1, E])
    w1T = din('w1T', [E, D, HID])
    w3T = din('w3T', [E, D, HID])
    w2T = din('w2T', [E, HID, D])
    sw1T = din('sw1T', [D, HID])
    sw3T = din('sw3T', [D, HID])
    sw2T = din('sw2T', [HID, D])
    slopes_i = din('slopes_rep', [1, H * 128], BF16)
    qrow_i = din('qrow', [1, NSB], BF16)
    alibik_i = din('alibi_k', [128, H * 128])
    flags_i = din('flags', [128, 128])
    canon_i = din('canon', [128, 4 * 512])
    blkidx_i = din('blkidx', [1, NS], U32)
    identB_i = din('identB', [128, 128], BF16)
    identF_i = din('identF', [128, 128])

    out_t = nc.dram_tensor('out_t', [D, OWN], F32, kind='ExternalOutput').ap()

    KI = 'ExternalOutput' if dbg else 'Internal'
    stats_d = nc.dram_tensor('stats_d', [2, S], F32, kind=KI).ap()
    wT_d = nc.dram_tensor('wT_d', [E, S], F32, kind=KI).ap()
    xn_d = nc.dram_tensor('xn_d', [D, S], BF16, kind=KI).ap()
    qT_d = nc.dram_tensor('qT_d', [H * 128, S], BF16, kind=KI).ap()
    kT_d = nc.dram_tensor('kT_d', [H * 128, S], BF16, kind=KI).ap()
    vtok_d = nc.dram_tensor('vtok_d', [H, 128, S // 128, 128], BF16,
                            kind=KI).ap()
    qTo_d = nc.dram_tensor('qTo_d', [H * 128, OWN], BF16, kind=KI).ap()
    xno_d = nc.dram_tensor('xno_d', [D, OWN], BF16, kind=KI).ap()
    wTo_d = nc.dram_tensor('wTo_d', [E, OWN], F32, kind=KI).ap()
    attn_d = nc.dram_tensor('attn_d', [D, OWN], BF16, kind=KI).ap()

    ones_r = nc.alloc_sbuf_tensor('ones_r', [1, 128], F32)
    ones_rb = nc.alloc_sbuf_tensor('ones_rb', [1, 128], BF16)
    ones_c = nc.alloc_sbuf_tensor('ones_c', [128, 1], F32)
    ones_cb = nc.alloc_sbuf_tensor('ones_cb', [128, 1], BF16)
    identB = nc.alloc_sbuf_tensor('identB_s', [128, 128], BF16)
    identF = nc.alloc_sbuf_tensor('identF_s', [128, 128], F32)
    blki_s = nc.alloc_sbuf_tensor('blki_s', [1, NS], U32)
    eb_row = nc.alloc_sbuf_tensor('eb_row', [1, E], F32)
    ebias_t = nc.alloc_sbuf_tensor('ebias_t', [128, E], F32)

    sch = Sched(nc)
    OP = sch.op
    blk_sv = []

    with ExitStack() as _st1:
        xb = _st1.enter_context(nc.sbuf_tensor('p0_xb', [128, 512 * 3], F32))
        sqb = _st1.enter_context(nc.sbuf_tensor('p0_sq', [128, 512 * 2], F32))
        gtb = _st1.enter_context(nc.sbuf_tensor('p0_gt', [128, E * 2], F32))
        rows = _st1.enter_context(nc.sbuf_tensor('p0_rows', [1, 512 * 7], F32))
        r_all = _st1.enter_context(nc.sbuf_tensor('p0_rall', [1, S], F32))
        mr_all = _st1.enter_context(nc.sbuf_tensor('p0_mrall', [1, S], F32))
        raw_sb = _st1.enter_context(nc.sbuf_tensor('p0_raws', [E, 512], F32))
        sraw_sb = _st1.enter_context(nc.sbuf_tensor('p0_srw', [E, 512], F32))
        sig_sb = _st1.enter_context(nc.sbuf_tensor('p0_sig', [E, 512], F32))
        G_sb = _st1.enter_context(nc.sbuf_tensor('p0_Gs', [1, E], F32))
        tt = _st1.enter_context(nc.sbuf_tensor('p0_tt', [128, 8 * 6], F32))
        m1c = _st1.enter_context(nc.sbuf_tensor('p0_m1', [128, 1], F32))
        m2c = _st1.enter_context(nc.sbuf_tensor('p0_m2', [128, 1], F32))
        wT_sb = _st1.enter_context(nc.sbuf_tensor('p0_wT', [E, S], F32))
        pSX = _st1.enter_context(nc.psum_tensor('p0_sx', [1, 512], F32))
        pSXQ = _st1.enter_context(nc.psum_tensor('p0_sxq', [1, 512], F32))
        pRAW = _st1.enter_context(nc.psum_tensor('p0_raw', [E, 512], F32))
        pG = _st1.enter_context(nc.psum_tensor('p0_pG', [1, E], F32))
        pEB = _st1.enter_context(nc.psum_tensor('p0_eb', [128, E], F32))
        pRB8 = _st1.enter_context(nc.psum_tensor('p0_rb8', [E, 512], F32))
        pTPa = _st1.enter_context(nc.psum_tensor('p0_tpa', [128, 8], F32))
        pTPb = _st1.enter_context(nc.psum_tensor('p0_tpb', [8, 128], F32))

        m_row = rows[0:1, 0:512]; msq_row = rows[0:1, 512:1024]
        m2_row = rows[0:1, 1024:1536]; var_row = rows[0:1, 1536:2048]
        sd_row = rows[0:1, 2048:2560]; negm_row = rows[0:1, 2560:3072]
        rcp_row = rows[0:1, 3072:3584]
        st_t = tt[:, 0:8]; sb_t = tt[:, 8:16]; eq_t = tt[:, 16:24]
        sbx_t = tt[:, 24:32]; top_t = tt[:, 32:40]; w_t = tt[:, 40:48]

        OP('sync', lambda Ee: Ee.dma_start(identB[:], identB_i),
           writes=['identB'], dma=True)
        OP('sync', lambda Ee: Ee.dma_start(identF[:], identF_i),
           writes=['identF'], dma=True)
        OP('sync', lambda Ee: Ee.dma_start(blki_s[:], blkidx_i),
           writes=['blki'], dma=True)
        OP('sync', lambda Ee: Ee.dma_start(eb_row[:], ebias),
           writes=['ebrow'], dma=True)

        def _vload(Ee):
            for s in range(NS):
                blk_sv.append(Ee.value_load(blki_s[0:1, s:s + 1]))
            return Ee.nop()
        OP('sync', _vload, reads=['blki'], tag='vload')

        OP('vector', lambda Ee: Ee.memset(ones_r[:], 1.0), writes=['ones_r'])
        OP('vector', lambda Ee: Ee.memset(ones_rb[:], 1.0), writes=['ones_rb'])
        OP('vector', lambda Ee: Ee.memset(ones_c[:], 1.0), writes=['ones_c'])
        OP('vector', lambda Ee: Ee.memset(ones_cb[:], 1.0), writes=['ones_cb'])
        OP('tensor', lambda Ee: Ee.matmul(pEB[:], ones_r[0:1, :],
                                        eb_row[0:1, 0:E],
                                        start=True, stop=True),
           reads=['ones_r', 'ebrow'], writes=['pEB'])
        OP('vector', lambda Ee: Ee.tensor_copy(ebias_t[:], pEB[:]),
           reads=['pEB'], writes=['ebias_t'])

        for t in range(T8):
            for d in range(DC):
                n = t * DC + d
                xs = xb[:, (n % 3) * 512:(n % 3 + 1) * 512]
                sq = sqb[:, (n % 2) * 512:(n % 2 + 1) * 512]
                gs = gtb[:, (n % 2) * E:(n % 2 + 1) * E]
                OP('sync', lambda Ee, xs=xs, d=d, t=t: Ee.dma_start(
                    xs, xT[d * 128:(d + 1) * 128, t * 512:(t + 1) * 512]),
                   writes=[('xb', n % 3)], dma=True)
                OP('gpsimd', lambda Ee, gs=gs, d=d: Ee.dma_start(
                    gs, gateT[d * 128:(d + 1) * 128, :]),
                   writes=[('gt', n % 2)], dma=True)
                OP('scalar', lambda Ee, sq=sq, xs=xs: Ee.activation(
                    sq, xs, AF.Square),
                   reads=[('xb', n % 3)], writes=[('sq', n % 2)])
                OP('tensor', lambda Ee, xs=xs, d=d: Ee.matmul(
                    pSX[:], ones_c[:], xs, start=(d == 0), stop=(d == DC - 1)),
                   reads=[('xb', n % 3), 'ones_c'], writes=['pSX'])
                OP('tensor', lambda Ee, sq=sq, d=d: Ee.matmul(
                    pSXQ[:], ones_c[:], sq, start=(d == 0),
                    stop=(d == DC - 1)),
                   reads=[('sq', n % 2), 'ones_c'], writes=['pSXQ'])
                OP('tensor', lambda Ee, xs=xs, gs=gs, d=d: Ee.matmul(
                    pRAW[:], gs, xs, start=(d == 0), stop=(d == DC - 1)),
                   reads=[('xb', n % 3), ('gt', n % 2)], writes=['pRAW'])
                if t == 0:
                    OP('tensor', lambda Ee, gs=gs, d=d: Ee.matmul(
                        pG[:], ones_c[:], gs,
                        start=(d == 0), stop=(d == DC - 1)),
                       reads=[('gt', n % 2), 'ones_c'], writes=['pG'])
            if t == 0:
                OP('vector', lambda Ee: Ee.tensor_copy(G_sb[:], pG[:]),
                   reads=['pG'], writes=['G'])
            OP('vector', lambda Ee: Ee.tensor_scalar(m_row, pSX[0:1, :],
                                                   1.0 / D, None, ALU.mult),
               reads=['pSX'], writes=['m_row'])
            OP('vector', lambda Ee: Ee.tensor_scalar(msq_row, pSXQ[0:1, :],
                                                   1.0 / D, None, ALU.mult),
               reads=['pSXQ'], writes=['msq_row'])
            OP('vector', lambda Ee: Ee.tensor_scalar(negm_row, m_row, -1.0,
                                                   None, ALU.mult),
               reads=['m_row'], writes=['negm_row'])
            OP('vector', lambda Ee: Ee.tensor_tensor(m2_row, m_row, m_row,
                                                   ALU.mult),
               reads=['m_row'], writes=['m2_row'])
            OP('vector', lambda Ee: Ee.scalar_tensor_tensor(
                var_row, msq_row, float(EPS), m2_row, ALU.add, ALU.subtract),
               reads=['msq_row', 'm2_row'], writes=['var_row'])
            OP('scalar', lambda Ee: Ee.activation(sd_row, var_row, AF.Sqrt),
               reads=['var_row'], writes=['sd_row'])
            OP('vector', lambda Ee: Ee.reciprocal(rcp_row, sd_row),
               reads=['sd_row'], writes=['rcp_row'])
            OP('vector', lambda Ee, t=t: Ee.tensor_copy(
                r_all[0:1, t * 512:(t + 1) * 512], rcp_row),
               reads=['rcp_row'], writes=['r_all'])
            OP('vector', lambda Ee, t=t: Ee.tensor_tensor(
                mr_all[0:1, t * 512:(t + 1) * 512], m_row, rcp_row, ALU.mult),
               reads=['rcp_row', 'm_row'], writes=['mr_all'])
            OP('tensor', lambda Ee: Ee.matmul(pRAW[:], G_sb[0:1, :], negm_row,
                                            start=False, stop=True,
                                            skip_group_check=True),
               reads=['G', 'negm_row'], writes=['pRAW'])
            OP('vector', lambda Ee: Ee.tensor_copy(raw_sb[:], pRAW[:]),
               reads=['pRAW'], writes=['raw_sb'])
            OP('tensor', lambda Ee: Ee.matmul(pRB8[:], ones_r[0:1, 0:E],
                                            rcp_row, start=True, stop=True),
               reads=['rcp_row', 'ones_r'], writes=['pRB8'])
            OP('vector', lambda Ee: Ee.tensor_tensor(sraw_sb[:], raw_sb[:],
                                                   pRB8[:], ALU.mult),
               reads=['raw_sb', 'pRB8'], writes=['sraw'])
            OP('scalar', lambda Ee: Ee.activation(sig_sb[:], sraw_sb[:],
                                                AF.Sigmoid),
               reads=['sraw'], writes=['sig'])
            for q in range(4):
                OP('tensor', lambda Ee, q=q: Ee.matmul(
                    pTPa[:], sig_sb[0:E, q * 128:(q + 1) * 128],
                    identF[0:E, 0:E], start=True, stop=True),
                   reads=['sig', 'identF'], writes=['pTPa'])
                OP('vector', lambda Ee: Ee.tensor_copy(st_t[:, 0:E],
                                                     pTPa[:, 0:E]),
                   reads=['pTPa'], writes=['st'])
                OP('vector', lambda Ee: Ee.tensor_tensor(sb_t[:, 0:E],
                                                       st_t[:, 0:E],
                                                       ebias_t[:, 0:E],
                                                       ALU.add),
                   reads=['st', 'ebias_t'], writes=['sb'])
                OP('vector', lambda Ee: Ee.reduce_max(m1c[:], sb_t[:, 0:E],
                                                    axis=AX),
                   reads=['sb'], writes=['m1'], fence=True)
                OP('vector', lambda Ee: Ee.tensor_scalar(
                    eq_t[:, 0:E], sb_t[:, 0:E], m1c[:, 0:1], None, ALU.is_ge),
                   reads=['sb', 'm1'], writes=['eq'])
                OP('vector', lambda Ee: Ee.scalar_tensor_tensor(
                    sbx_t[:, 0:E], eq_t[:, 0:E], NEG, sb_t[:, 0:E],
                    ALU.mult, ALU.add),
                   reads=['eq', 'sb'], writes=['sbx'])
                OP('vector', lambda Ee: Ee.reduce_max(m2c[:], sbx_t[:, 0:E],
                                                    axis=AX),
                   reads=['sbx'], writes=['m2'], fence=True)
                OP('vector', lambda Ee: Ee.tensor_scalar(
                    top_t[:, 0:E], sb_t[:, 0:E], m2c[:, 0:1], None,
                    ALU.is_ge),
                   reads=['sb', 'm2'], writes=['top'])
                OP('vector', lambda Ee: Ee.tensor_tensor(w_t[:, 0:E],
                                                       st_t[:, 0:E],
                                                       top_t[:, 0:E],
                                                       ALU.mult),
                   reads=['st', 'top'], writes=['w'])
                OP('tensor', lambda Ee: Ee.matmul(pTPb[:], w_t[:, 0:E],
                                                identF[:, 0:128],
                                                start=True, stop=True),
                   reads=['w', 'identF'], writes=['pTPb'])
                OP('vector', lambda Ee, t=t, q=q: Ee.tensor_copy(
                    wT_sb[:, t * 512 + q * 128: t * 512 + (q + 1) * 128],
                    pTPb[0:E, :]),
                   reads=['pTPb'], writes=['wTs'])
        OP('sync', lambda Ee: Ee.dma_start(stats_d[0:1, :], r_all[:]),
           reads=['r_all'], writes=[('std',)], dma=True)
        OP('sync', lambda Ee: Ee.dma_start(stats_d[1:2, :], mr_all[:]),
           reads=['mr_all'], writes=[('std',)], dma=True)
        OP('sync', lambda Ee: Ee.dma_start(wT_d, wT_sb[:]),
           reads=['wTs'], writes=[('wtd',)], dma=True)
        sch.drain()
        sch.emit('P0')
        if phase_limit <= 0:
            nc.finalize()
            return nc

    with ExitStack() as _st2:
        xnT = _st2.enter_context(nc.sbuf_tensor('p1_xn', [128, DC * S], BF16))
        rws = _st2.enter_context(nc.sbuf_tensor('p1_rws', [1, 512 * 4], BF16))
        rBf = _st2.enter_context(nc.sbuf_tensor('p1_rB', [128, S], BF16))
        mrBf = _st2.enter_context(nc.sbuf_tensor('p1_mrB', [128, S], BF16))
        xb = _st2.enter_context(nc.sbuf_tensor('p1_xb', [128, 512 * 3], F32))
        tmpb = _st2.enter_context(nc.sbuf_tensor('p1_tmp', [128, 512 * 2], F32))
        wb = _st2.enter_context(nc.sbuf_tensor('p1_wb', [128, DC * 128 * 2], BF16))
        stg = _st2.enter_context(nc.sbuf_tensor('p1_st', [128, min(4, T8) * 512 * 2], BF16))
        vst = _st2.enter_context(nc.sbuf_tensor('p1_vst', [128, 512 * 2], BF16))
        vtt = _st2.enter_context(nc.sbuf_tensor('p1_vtt', [128, S * 2], BF16))
        pA0 = _st2.enter_context(nc.psum_tensor('p1_a0', [128, 512], F32))
        pA1 = _st2.enter_context(nc.psum_tensor('p1_a1', [128, 512], F32))
        pA2 = _st2.enter_context(nc.psum_tensor('p1_a2', [128, 512], F32))
        pA3 = _st2.enter_context(nc.psum_tensor('p1_a3', [128, 512], F32))
        pRB = _st2.enter_context(nc.psum_tensor('p1_rb', [128, 512], F32))
        pMRB = _st2.enter_context(nc.psum_tensor('p1_mrb', [128, 512], F32))
        pTR0 = _st2.enter_context(nc.psum_tensor('p1_tr0', [128, 128], BF16))
        pTR1 = _st2.enter_context(nc.psum_tensor('p1_tr1', [128, 128], BF16))

        pAcc = [pA0, pA1, pA2, pA3]
        pTRl = [pTR0, pTR1]

        def xns(d, c0, c1):
            return xnT[:, d * S + c0: d * S + c1]

        for t in range(T8):
            rr = rws[0:1, (t % 2) * 512:(t % 2 + 1) * 512]
            mm_ = rws[0:1, 1024 + (t % 2) * 512:1024 + (t % 2 + 1) * 512]
            OP('gpsimd', lambda Ee, rr=rr, t=t: Ee.dma_start(
                rr, stats_d[0:1, t * 512:(t + 1) * 512]),
               writes=[('rr', t % 2)], dma=True)
            OP('gpsimd', lambda Ee, mm_=mm_, t=t: Ee.dma_start(
                mm_, stats_d[1:2, t * 512:(t + 1) * 512]),
               writes=[('mr', t % 2)], dma=True)
            OP('tensor', lambda Ee, rr=rr: Ee.matmul(
                pRB[:], ones_rb[0:1, :], rr, start=True, stop=True),
               reads=[('rr', t % 2), 'ones_rb'], writes=['pRB'])
            OP('vector', lambda Ee, t=t: Ee.tensor_copy(
                rBf[:, t * 512:(t + 1) * 512], pRB[:]),
               reads=['pRB'], writes=[('rBf', t)])
            OP('tensor', lambda Ee, mm_=mm_: Ee.matmul(
                pMRB[:], ones_rb[0:1, :], mm_, start=True, stop=True),
               reads=[('mr', t % 2), 'ones_rb'], writes=['pMRB'])
            OP('vector', lambda Ee, t=t: Ee.tensor_copy(
                mrBf[:, t * 512:(t + 1) * 512], pMRB[:]),
               reads=['pMRB'], writes=[('mrBf', t)])
        for d in range(DC):
            for t in range(T8):
                n = d * T8 + t
                xs = xb[:, (n % 3) * 512:(n % 3 + 1) * 512]
                tm = tmpb[:, (n % 2) * 512:(n % 2 + 1) * 512]
                OP('sync', lambda Ee, xs=xs, d=d, t=t: Ee.dma_start(
                    xs, xT[d * 128:(d + 1) * 128, t * 512:(t + 1) * 512]),
                   writes=[('xb', n % 3)], dma=True)
                OP('vector', lambda Ee, xs=xs, tm=tm, t=t: Ee.tensor_tensor(
                    tm, xs, rBf[:, t * 512:(t + 1) * 512], ALU.mult),
                   reads=[('xb', n % 3), ('rBf', t)],
                   writes=[('tm', n % 2)])
                OP('vector', lambda Ee, tm=tm, d=d, t=t:
                   Ee.tensor_tensor(xns(d, t * 512, (t + 1) * 512), tm,
                                   mrBf[:, t * 512:(t + 1) * 512],
                                   ALU.subtract),
                   reads=[('tm', n % 2), ('mrBf', t)], writes=[('xn', d)])
            OP('sync', lambda Ee, d=d: Ee.dma_start(
                xn_d[d * 128:(d + 1) * 128, :], xns(d, 0, S)),
               reads=[('xn', d)], writes=[('xnd',)], dma=True)

        PROJ = [('q', wqT, qT_d), ('k', wkT, kT_d), ('v', wvT, None)]
        G4 = min(4, T8)
        GW = G4 * 512
        TGRP = T8 // G4
        widx = 0
        vpend = []

        def v_tail(dh_, t_, ti_):
            vs = vst[:, (t_ % 2) * 512:(t_ % 2 + 1) * 512]
            OP('vector', lambda Ee, vs=vs, ti_=ti_:
               Ee.tensor_copy(vs, pAcc[ti_][:]),
               reads=[('pA', ti_)], writes=[('vst', t_ % 2)])
            for sub in range(4):
                tok = t_ * 4 + sub
                pT = pTRl[sub % 2]
                OP('tensor', lambda Ee, vs=vs, sub=sub, pT=pT:
                   Ee.transpose(pT[:], vs[:, sub * 128:(sub + 1) * 128],
                                identB[:]),
                   reads=[('vst', t_ % 2), 'identB'],
                   writes=[('pTR', sub % 2)])
                OP('vector', lambda Ee, tok=tok, pT=pT, dh_=dh_:
                   Ee.tensor_copy(
                       vtt[:, (dh_ % 2) * S + tok * 128:
                           (dh_ % 2) * S + (tok + 1) * 128],
                       pT[:]),
                   reads=[('pTR', sub % 2)], writes=[('vtt', dh_ % 2)])
            if t_ == T8 - 1:
                OP('sync', lambda Ee, dh_=dh_: Ee.dma_start(
                    vtok_d[dh_],
                    vtt[:, (dh_ % 2) * S:(dh_ % 2 + 1) * S].rearrange(
                        'p (a f) -> p a f', f=128)),
                   reads=[('vtt', dh_ % 2)], writes=[('vtokd',)], dma=True)

        for wname, wsrc, dst in PROJ:
            wsrc_r = wsrc.rearrange('(c p) f -> p c f', p=128)
            for dh in range(H):
                wi = widx % 2
                widx += 1
                wtile = wb[:, wi * DC * 128:(wi + 1) * DC * 128]
                OP('gpsimd', lambda Ee, wtile=wtile, wsrc_r=wsrc_r, dh=dh:
                   Ee.dma_start(wtile.rearrange('p (c f) -> p c f', c=DC),
                               wsrc_r[:, :, dh * 128:(dh + 1) * 128]),
                   writes=[('wb', wi)], dma=True)
                for tg in range(TGRP):
                    if wname != 'v':
                        for d in range(DC):
                            for ti in range(G4):
                                t = tg * G4 + ti
                                OP('tensor',
                                   lambda Ee, wtile=wtile, d=d, t=t, ti=ti:
                                   Ee.matmul(pAcc[ti][:],
                                            wtile[:, d * 128:(d + 1) * 128],
                                            xns(d, t * 512, (t + 1) * 512),
                                            start=(d == 0),
                                            stop=(d == DC - 1)),
                                   reads=[('wb', wi), ('xn', d)],
                                   writes=[('pA', ti)])
                        for ti in range(G4):
                            sg = stg[:, (tg % 2) * GW + ti * 512:
                                     (tg % 2) * GW + (ti + 1) * 512]
                            OP('vector', lambda Ee, sg=sg, ti=ti:
                               Ee.tensor_copy(sg, pAcc[ti][:]),
                               reads=[('pA', ti)], writes=[('stg', tg % 2)])
                        OP('sync', lambda Ee, dst=dst, dh=dh, tg=tg:
                           Ee.dma_start(
                               dst[dh * 128:(dh + 1) * 128,
                                   tg * GW:(tg + 1) * GW],
                               stg[:, (tg % 2) * GW:(tg % 2) * GW + GW]),
                           reads=[('stg', tg % 2)],
                           writes=[(wname + 'Td', tg)], dma=True)
                    else:
                        for ti in range(G4):
                            t = tg * G4 + ti
                            for d in range(DC):
                                OP('tensor',
                                   lambda Ee, wtile=wtile, d=d, t=t, ti=ti:
                                   Ee.matmul(pAcc[ti][:],
                                            wtile[:, d * 128:(d + 1) * 128],
                                            xns(d, t * 512, (t + 1) * 512),
                                            start=(d == 0),
                                            stop=(d == DC - 1)),
                                   reads=[('wb', wi), ('xn', d)],
                                   writes=[('pA', ti)])
                            if vpend:
                                v_tail(*vpend.pop(0))
                            vpend.append((dh, t, ti))
        while vpend:
            v_tail(*vpend.pop(0))
        qkeys = [('qTd', tg) for tg in range(TGRP)]
        xkeys = [('xnd',)]
        for s in range(NS):
            OP('sync', lambda Ee, s=s: Ee.dma_start(
                qTo_d.rearrange('(c p) f -> p c f', p=128)[
                    :, :, s * 512:(s + 1) * 512],
                qT_d.rearrange('(c p) f -> p c f', p=128)[
                    :, :, bass.ts(blk_sv[s], 512)]),
               reads=qkeys, writes=[('qTod',)], dma=True)
            OP('sync', lambda Ee, s=s: Ee.dma_start(
                xno_d.rearrange('(c p) f -> p c f', p=128)[
                    :, :, s * 512:(s + 1) * 512],
                xn_d.rearrange('(c p) f -> p c f', p=128)[
                    :, :, bass.ts(blk_sv[s], 512)]),
               reads=xkeys, writes=[('xnod',)], dma=True)
            OP('sync', lambda Ee, s=s: Ee.dma_start(
                wTo_d[:, s * 512:(s + 1) * 512],
                wT_d[:, bass.ts(blk_sv[s], 512)]),
               writes=[('wTod',)], dma=True)
        sch.drain()
        sch.emit('P1')
        if phase_limit <= 2:
            nc.finalize()
            return nc

    KTmax = max(cfg.KT)
    with ExitStack() as _st3:
        alk = _st3.enter_context(nc.sbuf_tensor('p2_alk', [128, H * 128], F32))
        slp = _st3.enter_context(nc.sbuf_tensor('p2_slp', [1, H * 128], BF16))
        qr = _st3.enter_context(nc.sbuf_tensor('p2_qr', [1, NSB], BF16))
        can = _st3.enter_context(nc.sbuf_tensor('p2_can', [128, 4 * 512], F32))
        flg = _st3.enter_context(nc.sbuf_tensor('p2_flg', [128, 128], F32))
        kb = _st3.enter_context(nc.sbuf_tensor('p2_kb', [128, KTmax * 128 * 3], BF16))
        vb = _st3.enter_context(nc.sbuf_tensor('p2_vb', [128, KTmax * 128 * 3], BF16))
        pb = _st3.enter_context(nc.sbuf_tensor('p2_pb', [128, 512 * 4], BF16))
        scb = _st3.enter_context(nc.sbuf_tensor('p2_scb', [128, 512 * 4], F32))
        dnrow = _st3.enter_context(nc.sbuf_tensor('p2_dn', [1, 512 * 2], F32))
        rcrow = _st3.enter_context(nc.sbuf_tensor('p2_rc', [1, 512 * 2], BF16))
        rbsb = _st3.enter_context(nc.sbuf_tensor('p2_rbs', [128, 512 * 2], F32))
        pBsb = _st3.enter_context(nc.sbuf_tensor('p2_pBs', [128, 512 * 2], BF16))
        zsb = _st3.enter_context(nc.sbuf_tensor('p2_zsb', [128, H * 512 * 2], BF16))
        qsbuf = _st3.enter_context(nc.sbuf_tensor('p2_qs', [128, H * 512 * 2], BF16))
        wob = _st3.enter_context(nc.sbuf_tensor('p2_wo', [128, H * 128 * 2], BF16))
        ast = _st3.enter_context(nc.sbuf_tensor('p2_ast', [128, 512 * 2], BF16))
        pS0 = _st3.enter_context(nc.psum_tensor('p2_s0', [128, 512], F32))
        pS1 = _st3.enter_context(nc.psum_tensor('p2_s1', [128, 512], F32))
        pS2 = _st3.enter_context(nc.psum_tensor('p2_s2', [128, 512], F32))
        pZ0 = _st3.enter_context(nc.psum_tensor('p2_z0', [128, 512], F32))
        pZ1 = _st3.enter_context(nc.psum_tensor('p2_z1', [128, 512], F32))
        pDN = _st3.enter_context(nc.psum_tensor('p2_dnp', [1, 512], F32))
        pWO = _st3.enter_context(nc.psum_tensor('p2_wop', [128, 512], F32))
        pRB2 = _st3.enter_context(nc.psum_tensor('p2_rb2', [128, 512], F32))

        pS = [pS0, pS1, pS2]
        pZl = [pZ0, pZ1]
        wo_r = woT.rearrange('(c p) f -> p c f', p=128)

        def wo_block(sb, dm):
            wt = wob[:, (dm % 2) * H * 128:(dm % 2 + 1) * H * 128]
            OP('gpsimd', lambda Ee, wt=wt, dm=dm: Ee.dma_start(
                wt.rearrange('p (c f) -> p c f', c=H),
                wo_r[:, :, dm * 128:(dm + 1) * 128]),
               writes=[('wo', dm % 2)], dma=True)
            for hh in range(H):
                OP('tensor', lambda Ee, wt=wt, hh=hh, sb=sb:
                   Ee.matmul(pWO[:], wt[:, hh * 128:(hh + 1) * 128],
                            zsb[:, (sb % 2) * H * 512 + hh * 512:
                                (sb % 2) * H * 512 + (hh + 1) * 512],
                            start=(hh == 0), stop=(hh == H - 1)),
                   reads=[('wo', dm % 2), ('zsb', sb % 2, hh)],
                   writes=['pWO'])
            at = ast[:, (dm % 2) * 512:(dm % 2 + 1) * 512]
            OP('vector', lambda Ee, at=at: Ee.tensor_copy(at, pWO[:]),
               reads=['pWO'], writes=[('astw', dm % 2)])
            OP('sync', lambda Ee, at=at, dm=dm, sb=sb: Ee.dma_start(
                attn_d[dm * 128:(dm + 1) * 128, sb * 512:(sb + 1) * 512], at),
               reads=[('astw', dm % 2)], writes=[('attnd',)], dma=True)

        OP('sync', lambda Ee: Ee.dma_start(alk[:], alibik_i), writes=['alk'],
           dma=True)
        OP('sync', lambda Ee: Ee.dma_start(slp[:], slopes_i), writes=['slp'],
           dma=True)
        OP('sync', lambda Ee: Ee.dma_start(qr[:], qrow_i), writes=['qr'],
           dma=True)
        OP('sync', lambda Ee: Ee.dma_start(can[:], canon_i), writes=['can'],
           dma=True)
        OP('sync', lambda Ee: Ee.dma_start(flg[:], flags_i), writes=['flg'],
           dma=True)

        for s in range(NS):
            KT = cfg.KT[s]
            qsb = qsbuf[:, (s % 2) * H * 512:(s % 2) * H * 512 + H * 512]
            OP('sync', lambda Ee, qsb=qsb, s=s: Ee.dma_start(
                qsb.rearrange('p (c f) -> p c f', c=H),
                qTo_d.rearrange('(c p) f -> p c f', p=128)[
                    :, :, s * 512:(s + 1) * 512]),
               writes=[('qs', s % 2)], dma=True)
            for h in range(H):
                u = s * H + h
                kbu = kb[:, (u % 3) * KTmax * 128:
                         (u % 3) * KTmax * 128 + KT * 128]
                vbu = vb[:, (u % 3) * KTmax * 128:
                         (u % 3) * KTmax * 128 + KT * 128]
                qbu = qsb[:, h * 512:(h + 1) * 512]
                OP('sync', lambda Ee, kbu=kbu, h=h, KT=KT: Ee.dma_start(
                    kbu, kT_d[h * 128:(h + 1) * 128, 0:KT * 128]),
                   writes=[('kb', u % 3)], dma=True)
                OP('gpsimd', lambda Ee, vbu=vbu, h=h, KT=KT: Ee.dma_start(
                    vbu.rearrange('p (a f) -> p a f', f=128),
                    vtok_d[h, :, 0:KT, :]),
                   writes=[('vb', u % 3)], dma=True)
                pZ = pZl[u % 2]
                cand = (cfg.own[0][s], cfg.own[1][s])
                pBu = pBsb[:, (u % 2) * 512:(u % 2 + 1) * 512]

                OP('tensor', lambda Ee, h=h, s=s: Ee.matmul(
                    pRB2[:], slp[0:1, h * 128:(h + 1) * 128],
                    qr[0:1, s * 512:(s + 1) * 512], start=True, stop=True),
                   reads=['slp', 'qr'], writes=['pRB2'])
                OP('vector', lambda Ee, pBu=pBu: Ee.tensor_copy(pBu, pRB2[:]),
                   reads=['pRB2'], writes=[('pBs', u % 2)])

                def d_qk(kc):
                    n3 = kc % 3
                    sb_ = pS[n3]
                    OP('tensor', lambda Ee, sb_=sb_, kbu=kbu, qbu=qbu, kc=kc:
                       Ee.matmul(sb_[:], kbu[:, kc * 128:(kc + 1) * 128], qbu,
                                start=True, stop=True),
                       reads=[('kb', u % 3), ('qs', s % 2)],
                       writes=[('pS', n3)])

                def d_clip(kc):
                    n3 = kc % 3
                    n4 = kc % 4
                    sb_ = pS[n3]
                    scs = scb[:, n4 * 512:(n4 + 1) * 512]
                    OP('vector', lambda Ee, sb_=sb_, scs=scs, pBu=pBu:
                       Ee.scalar_tensor_tensor(
                           scs, sb_[:], CLIP, pBu, ALU.min, ALU.add),
                       reads=[('pS', n3), ('pBs', u % 2)],
                       writes=[('sc', n4)])
                    if kc // 4 in cand:
                        OP('vector', lambda Ee, scs=scs, s=s, kc=kc:
                           Ee.scalar_tensor_tensor(
                               scs,
                               can[:, (kc % 4) * 512:(kc % 4 + 1) * 512],
                               flg[:, s * 32 + kc: s * 32 + kc + 1], scs,
                               ALU.mult, ALU.add),
                           reads=[('sc', n4), 'can', 'flg'],
                           writes=[('sc', n4)])

                def d_exp(kc):
                    n4 = kc % 4
                    scs = scb[:, n4 * 512:(n4 + 1) * 512]
                    pbu = pb[:, n4 * 512:(n4 + 1) * 512]
                    OP('scalar', lambda Ee, scs=scs, pbu=pbu, h=h, s=s, kc=kc:
                       Ee.activation(pbu, scs, AF.Exp,
                                    bias=alk[:, h * 128 + s * 32 + kc:
                                             h * 128 + s * 32 + kc + 1],
                                    scale=SC),
                       reads=[('sc', n4), 'alk'], writes=[('pb', n4)])

                def d_dnav(kc):
                    n4 = kc % 4
                    pbu = pb[:, n4 * 512:(n4 + 1) * 512]
                    OP('tensor', lambda Ee, pbu=pbu, kc=kc, KT=KT:
                       Ee.matmul(
                        pDN[:], ones_cb[:], pbu,
                        start=(kc == 0), stop=(kc == KT - 1)),
                       reads=[('pb', n4), 'ones_cb'],
                       writes=[('pDN',)])
                    OP('tensor', lambda Ee, pZ=pZ, pbu=pbu, vbu=vbu, kc=kc,
                       KT=KT:
                       Ee.matmul(pZ[:], vbu[:, kc * 128:(kc + 1) * 128], pbu,
                                start=(kc == 0), stop=(kc == KT - 1)),
                       reads=[('pb', n4), ('vb', u % 3)],
                       writes=[('pZ', u % 2)])

                for i in range(KT + 6):
                    if i < KT:
                        d_qk(i)
                    if 2 <= i < KT + 2:
                        d_clip(i - 2)
                    if 4 <= i < KT + 4:
                        d_exp(i - 4)
                    if i >= 6:
                        d_dnav(i - 6)
                dnr = dnrow[0:1, (u % 2) * 512:(u % 2 + 1) * 512]
                rcr = rcrow[0:1, (u % 2) * 512:(u % 2 + 1) * 512]
                rbs = rbsb[:, (u % 2) * 512:(u % 2 + 1) * 512]
                OP('vector', lambda Ee, dnr=dnr:
                   Ee.tensor_copy(dnr, pDN[:]),
                   reads=[('pDN',)], writes=[('dnr', u % 2)])

                def _recip(Ee, dnr=dnr, rcr=rcr):
                    with nc.allow_low_precision(reason='softmax 1/DN bf16'):
                        return Ee.reciprocal(rcr, dnr)
                OP('vector', _recip,
                   reads=[('dnr', u % 2)], writes=[('rcr', u % 2)])
                if s >= 1:
                    wo_block(s - 1, h)
                OP('tensor', lambda Ee, rcr=rcr: Ee.matmul(
                    pRB2[:], ones_rb[0:1, :], rcr, start=True, stop=True),
                   reads=[('rcr', u % 2), 'ones_rb'], writes=['pRB2'])
                OP('vector', lambda Ee, rbs=rbs: Ee.tensor_copy(rbs, pRB2[:]),
                   reads=['pRB2'], writes=[('rbs', u % 2)])
                OP('vector', lambda Ee, pZ=pZ, rbs=rbs, s=s, h=h:
                   Ee.tensor_tensor(
                       zsb[:, (s % 2) * H * 512 + h * 512:
                           (s % 2) * H * 512 + (h + 1) * 512],
                       pZ[:], rbs, ALU.mult),
                   reads=[('pZ', u % 2), ('rbs', u % 2)],
                   writes=[('zsb', s % 2, h)])
        for dm in range(DC):
            wo_block(NS - 1, dm)
        sch.drain()
        sch.emit('P2')
        if phase_limit <= 3:
            nc.finalize()
            return nc

    NT = NS
    with ExitStack() as _st5:
        xno = _st5.enter_context(nc.sbuf_tensor('p5_xn', [128, DC * NSB], BF16))
        acc = _st5.enter_context(nc.sbuf_tensor('p5_acc', [128, DC * NSB], BF16))

        def xos(d):
            return xno[:, d * NSB:(d + 1) * NSB]

        with ExitStack() as _st6:
            atb = _st6.enter_context(nc.sbuf_tensor('p5_at', [128, 512 * 2], BF16))
            xob = _st6.enter_context(nc.sbuf_tensor('p5_xo', [128, 512 * 2], F32))
            hx = _st6.enter_context(nc.sbuf_tensor('p5_hx', [128, HC * NSB], BF16))
            w13 = _st6.enter_context(nc.sbuf_tensor('p5_w13', [128, DC * 128 * 4], BF16))
            wrow = _st6.enter_context(nc.sbuf_tensor('p5_wrow', [1, NSB], BF16))
            w2b = _st6.enter_context(nc.sbuf_tensor('p5_w2', [128, HC * 128 * 2], BF16))
            wB = _st6.enter_context(nc.sbuf_tensor('p5_wB', [128, NSB], F32))
            gt = _st6.enter_context(nc.sbuf_tensor('p5_gt', [128, 512 * 2], F32))
            slb = _st6.enter_context(nc.sbuf_tensor('p5_sl', [128, 512 * 2], F32))
            gP0 = _st6.enter_context(nc.psum_tensor('p5_g0', [128, 512], F32))
            gP1 = _st6.enter_context(nc.psum_tensor('p5_g1', [128, 512], F32))
            gP2 = _st6.enter_context(nc.psum_tensor('p5_g2', [128, 512], F32))
            gP3 = _st6.enter_context(nc.psum_tensor('p5_g3', [128, 512], F32))
            uP0 = _st6.enter_context(nc.psum_tensor('p5_u0', [128, 512], F32))
            uP1 = _st6.enter_context(nc.psum_tensor('p5_u1', [128, 512], F32))
            uP2 = _st6.enter_context(nc.psum_tensor('p5_u2', [128, 512], F32))
            uP3 = _st6.enter_context(nc.psum_tensor('p5_u3', [128, 512], F32))

            gP = [gP0, gP1, gP2, gP3]
            uP = [uP0, uP1, uP2, uP3]

            OP('sync', lambda Ee: Ee.dma_start(
                xno.rearrange('p (c f) -> p c f', c=DC),
                xno_d.rearrange('(c p) f -> p c f', p=128)),
               writes=[('xno', d) for d in range(DC)], dma=True)

            for e in range(E + 1):
                shared = (e == E)
                if not shared:
                    OP('gpsimd', lambda Ee, e=e: Ee.dma_start(
                        wrow[0:1, :], wTo_d[e:e + 1, :]),
                       writes=['wrow'], dma=True)
                    for t in range(NT):
                        OP('tensor', lambda E_, t=t: E_.matmul(
                            gP0[:], ones_rb[0:1, :],
                            wrow[0:1, t * 512:(t + 1) * 512],
                            start=True, stop=True),
                           reads=['wrow', 'ones_rb'], writes=[('gP', 0)])
                        OP('vector', lambda E_, t=t: E_.tensor_copy(
                            wB[:, t * 512:(t + 1) * 512], gP0[:]),
                           reads=[('gP', 0)], writes=['wB'])
                for hc in range(HC):
                    wi = (e * HC + hc) % 2
                    wt1 = w13[:, wi * DC * 128:(wi + 1) * DC * 128]
                    wt3 = w13[:, (2 + wi) * DC * 128:(3 + wi) * DC * 128]
                    src1 = ((sw1T if shared else None) or None)
                    if shared:
                        s1r = sw1T.rearrange('(c p) f -> p c f', p=128)
                        s3r = sw3T.rearrange('(c p) f -> p c f', p=128)
                    else:
                        s1r = w1T[e].rearrange('(c p) f -> p c f', p=128)
                        s3r = w3T[e].rearrange('(c p) f -> p c f', p=128)
                    OP('gpsimd', lambda E_, wt1=wt1, s1r=s1r, hc=hc:
                       E_.dma_start(wt1.rearrange('p (c f) -> p c f', c=DC),
                                    s1r[:, :, hc * 128:(hc + 1) * 128]),
                       writes=[('w13a', wi)], dma=True)
                    OP('gpsimd', lambda E_, wt3=wt3, s3r=s3r, hc=hc:
                       E_.dma_start(wt3.rearrange('p (c f) -> p c f', c=DC),
                                    s3r[:, :, hc * 128:(hc + 1) * 128]),
                       writes=[('w13b', wi)], dma=True)
                    for d in range(DC):
                        for t in range(NT):
                            OP('tensor', lambda E_, wt1=wt1, d=d, t=t:
                               E_.matmul(gP[t][:],
                                         wt1[:, d * 128:(d + 1) * 128],
                                         xos(d)[:, t * 512:(t + 1) * 512],
                                         start=(d == 0), stop=(d == DC - 1)),
                               reads=[('w13a', wi), ('xno', d)],
                               writes=[('gP', t)])
                        for t in range(NT):
                            OP('tensor', lambda E_, wt3=wt3, d=d, t=t:
                               E_.matmul(uP[t][:],
                                         wt3[:, d * 128:(d + 1) * 128],
                                         xos(d)[:, t * 512:(t + 1) * 512],
                                         start=(d == 0), stop=(d == DC - 1)),
                               reads=[('w13b', wi), ('xno', d)],
                               writes=[('uP', t)])
                    for t in range(NT):
                        gts = gt[:, (t % 2) * 512:(t % 2 + 1) * 512]
                        sls = slb[:, (t % 2) * 512:(t % 2 + 1) * 512]
                        if shared:
                            OP('scalar', lambda E_, sls=sls, t=t:
                               E_.activation(sls, gP[t][:], AF.Silu),
                               reads=[('gP', t)], writes=[('sl', t % 2)])
                        else:
                            OP('vector', lambda E_, gts=gts, t=t:
                               E_.tensor_tensor(gts, gP[t][:],
                                                wB[:, t * 512:(t + 1) * 512],
                                                ALU.mult),
                               reads=[('gP', t), 'wB'], writes=[('gt', t % 2)])
                            OP('scalar', lambda E_, sls=sls, gts=gts:
                               E_.activation(sls, gts, AF.Silu),
                               reads=[('gt', t % 2)], writes=[('sl', t % 2)])
                        OP('vector', lambda E_, sls=sls, hc=hc, t=t:
                           E_.tensor_tensor(
                               hx[:, hc * NSB + t * 512:
                                  hc * NSB + (t + 1) * 512],
                               sls, uP[t][:], ALU.mult),
                           reads=[('sl', t % 2), ('uP', t)],
                           writes=[('hx', hc)])
                    if not shared:
                        OP('vector', lambda E_, hc=hc: E_.tensor_tensor(
                            hx[:, hc * NSB:(hc + 1) * NSB],
                            hx[:, hc * NSB:(hc + 1) * NSB],
                            wB[:], ALU.mult),
                           reads=[('hx', hc), 'wB'], writes=[('hx', hc)])
                for dm in range(DC):
                    wi = (e * DC + dm) % 2
                    wt2 = w2b[:, wi * HC * 128:(wi + 1) * HC * 128]
                    if shared:
                        s2r = sw2T.rearrange('(c p) f -> p c f', p=128)
                    else:
                        s2r = w2T[e].rearrange('(c p) f -> p c f', p=128)
                    OP('gpsimd', lambda E_, wt2=wt2, s2r=s2r, dm=dm:
                       E_.dma_start(wt2.rearrange('p (c f) -> p c f', c=HC),
                                    s2r[:, :, dm * 128:(dm + 1) * 128]),
                       writes=[('w2b', wi)], dma=True)
                    for t in range(NT):
                        if shared:
                            n = dm * NT + t
                            at = atb[:, (n % 2) * 512:(n % 2 + 1) * 512]
                            xo = xob[:, (n % 2) * 512:(n % 2 + 1) * 512]
                            OP('gpsimd', lambda Ee, at=at, dm=dm, t=t:
                               Ee.dma_start(
                                   at, attn_d[dm * 128:(dm + 1) * 128,
                                              t * 512:(t + 1) * 512]),
                               writes=[('at', n % 2)], dma=True)
                            OP('gpsimd', lambda Ee, xo=xo, dm=dm, t=t:
                               Ee.dma_start(
                                   xo, xTo[dm * 128:(dm + 1) * 128,
                                           t * 512:(t + 1) * 512]),
                               writes=[('xo', n % 2)], dma=True)
                        for hc in range(HC):
                            OP('tensor', lambda E_, wt2=wt2, hc=hc, t=t:
                               E_.matmul(gP[t][:],
                                         wt2[:, hc * 128:(hc + 1) * 128],
                                         hx[:, hc * NSB + t * 512:
                                            hc * NSB + (t + 1) * 512],
                                         start=(hc == 0),
                                         stop=(hc == HC - 1)),
                               reads=[('w2b', wi), ('hx', hc)],
                               writes=[('gP', t)])
                        aslice = acc[:, dm * NSB + t * 512:
                                     dm * NSB + (t + 1) * 512]
                        if e == 0:
                            OP('vector', lambda E_, aslice=aslice, t=t:
                               E_.tensor_copy(aslice, gP[t][:]),
                               reads=[('gP', t)], writes=[('acc', dm)])
                        elif not shared:
                            OP('vector', lambda E_, aslice=aslice, t=t:
                               E_.tensor_tensor(aslice, aslice, gP[t][:],
                                                ALU.add),
                               reads=[('gP', t), ('acc', dm)],
                               writes=[('acc', dm)])
                        else:
                            n = dm * NT + t
                            at = atb[:, (n % 2) * 512:(n % 2 + 1) * 512]
                            xo = xob[:, (n % 2) * 512:(n % 2 + 1) * 512]
                            o12 = gt[:, (n % 2) * 512:(n % 2 + 1) * 512]
                            OP('vector', lambda E_, o12=o12, aslice=aslice,
                               at=at:
                               E_.tensor_tensor(o12, aslice, at, ALU.add),
                               reads=[('at', n % 2), ('acc', dm)],
                               writes=[('o12', n % 2), ('gt', n % 2)])
                            OP('vector', lambda E_, o12=o12, xo=xo:
                               E_.tensor_tensor(o12, o12, xo, ALU.add),
                               reads=[('xo', n % 2), ('o12', n % 2)],
                               writes=[('o12', n % 2)])
                            OP('vector', lambda E_, o12=o12, t=t:
                               E_.tensor_tensor(o12, o12, gP[t][:], ALU.add),
                               reads=[('gP', t), ('o12', n % 2)],
                               writes=[('o12', n % 2)])
                            OP('sync', lambda Ee, o12=o12, dm=dm, t=t:
                               Ee.dma_start(
                                   out_t[dm * 128:(dm + 1) * 128,
                                         t * 512:(t + 1) * 512], o12),
                               reads=[('o12', n % 2)], writes=[('outw',)],
                               dma=True)
            sch.drain()
            sch.emit('P5a')

    nc.finalize()
    return nc



_CACHE = {}


def _get_nc(cfg):
    key = (cfg.D, cfg.S, cfg.H, cfg.HID, cfg.E)
    if key not in _CACHE:
        _CACHE[key] = build_nc(cfg)
    return _CACHE[key]


def make_core_inputs(cfg, c, x, Wq, Wk, Wv, Wo, gate, expert_bias,
                     w1, w2, w3, shw1, shw2, shw3):
    b, p = c // 2, c % 2
    cst = make_core_consts(cfg, p)
    xb = np.asarray(x[b], dtype=np.float32)
    own = cfg.own[p]
    x_own = np.concatenate([xb[g * BLK:(g + 1) * BLK] for g in own], axis=0)
    d = dict(
        xT=np.ascontiguousarray(xb.T),
        xTo=np.ascontiguousarray(x_own.T),
        wqT=np.ascontiguousarray(np.asarray(Wq, np.float32).T),
        wkT=np.ascontiguousarray(np.asarray(Wk, np.float32).T),
        wvT=np.ascontiguousarray(np.asarray(Wv, np.float32).T),
        woT=np.ascontiguousarray(np.asarray(Wo, np.float32).T),
        gateT=np.ascontiguousarray(np.asarray(gate, np.float32).T),
        ebias=np.ascontiguousarray(np.asarray(expert_bias,
                                              np.float32)[None, :]),
        w1T=np.ascontiguousarray(np.transpose(np.asarray(w1, np.float32),
                                              (0, 2, 1))),
        w3T=np.ascontiguousarray(np.transpose(np.asarray(w3, np.float32),
                                              (0, 2, 1))),
        w2T=np.ascontiguousarray(np.transpose(np.asarray(w2, np.float32),
                                              (0, 2, 1))),
        sw1T=np.ascontiguousarray(np.asarray(shw1, np.float32).T),
        sw3T=np.ascontiguousarray(np.asarray(shw3, np.float32).T),
        sw2T=np.ascontiguousarray(np.asarray(shw2, np.float32).T),
    )
    d.update(cst)
    return d


def kernel(L0_attn_resid_pre, L0_attn_Wq, L0_attn_Wk, L0_attn_Wv, L0_attn_Wo,
           L0_ffn_gate, L0_ffn_expert_bias, L0_ffn_w1, L0_ffn_w2, L0_ffn_w3,
           L0_ffn_shared_w1, L0_ffn_shared_w2, L0_ffn_shared_w3):
    cfg = Cfg()
    x = np.asarray(L0_attn_resid_pre, dtype=np.float32)
    nc = _get_nc(cfg)
    in_maps = [make_core_inputs(cfg, c, x, L0_attn_Wq, L0_attn_Wk, L0_attn_Wv,
                                L0_attn_Wo, L0_ffn_gate, L0_ffn_expert_bias,
                                L0_ffn_w1, L0_ffn_w2, L0_ffn_w3,
                                L0_ffn_shared_w1, L0_ffn_shared_w2,
                                L0_ffn_shared_w3)
               for c in range(8)]
    res = run_bass_kernel_spmd(nc, in_maps, core_ids=list(range(8)))
    out = np.empty((cfg.B, cfg.S, cfg.D), dtype=np.float32)
    for c in range(8):
        b, p = c // 2, c % 2
        ot = res.results[c]['out_t']
        for s, g in enumerate(cfg.own[p]):
            out[b, g * BLK:(g + 1) * BLK, :] = \
                np.asarray(ot[:, s * BLK:(s + 1) * BLK]).T
    return out
